# version 07
# baseline (speedup 1.0000x reference)
import numpy as np
import ml_dtypes
from contextlib import ExitStack
import concourse.bass as bass
import concourse.mybir as mybir
from concourse.bass_utils import run_bass_kernel_spmd

F32 = mybir.dt.float32
BF16 = mybir.dt.bfloat16
AF = mybir.ActivationFunctionType
ALU = mybir.AluOpType
AX = mybir.AxisListType

C = 2048
T = 2048
NCH = 16
EPS = 1e-5


def build_nc(debug=False):
    nc = bass.Bass("TRN2", target_bir_lowering=False)
    dbg = "ExternalOutput" if debug else "Internal"

    x_h = nc.dram_tensor("x", [C, T], F32, kind="ExternalInput").ap()
    wx_h = nc.dram_tensor("wx", [16, 128, 16, 128], F32, kind="ExternalInput").ap()
    ww_h = nc.dram_tensor("ww", [16, 128, 16, 128], F32, kind="ExternalInput").ap()
    wk_h = nc.dram_tensor("wk", [16, 128, 16, 128], F32, kind="ExternalInput").ap()
    wv_h = nc.dram_tensor("wv", [16, 128, 16, 128], F32, kind="ExternalInput").ap()
    wr_h = nc.dram_tensor("wr", [16, 128, 16, 128], F32, kind="ExternalInput").ap()
    wo_h = nc.dram_tensor("wo", [16, 128, 16, 128], F32, kind="ExternalInput").ap()
    gate_h = nc.dram_tensor("gate", [16, 128, 8], F32, kind="ExternalInput").ap()
    gb_h = nc.dram_tensor("gb", [1, 8], F32, kind="ExternalInput").ap()
    lns_h = nc.dram_tensor("lns", [128, 16], F32, kind="ExternalInput").ap()
    lnb_h = nc.dram_tensor("lnb", [128, 16], F32, kind="ExternalInput").ap()
    bw_h = nc.dram_tensor("bwv", [128, 16], F32, kind="ExternalInput").ap()
    gws_h = nc.dram_tensor("gws", [44, 128, 16, 128], F32, kind="ExternalInput").ap()
    dws_h = nc.dram_tensor("dws", [16, 128, 22, 128], F32, kind="ExternalInput").ap()
    upw_h = nc.dram_tensor("upw", [8, 22, 128, 16, 128], F32, kind="ExternalInput").ap()
    dnw_h = nc.dram_tensor("dnw", [8, 16, 128, 11, 128], F32, kind="ExternalInput").ap()
    out_h = nc.dram_tensor("out", [C, T], F32, kind="ExternalOutput").ap()

    k_scr = nc.dram_tensor("k_scr", [NCH, 128, 16, 128], BF16, kind=dbg).ap()
    v_scr = nc.dram_tensor("v_scr", [NCH, 128, 16, 128], BF16, kind=dbg).ap()
    r_scr = nc.dram_tensor("r_scr", [NCH, 128, 16, 128], BF16, kind=dbg).ap()
    sg_scr = nc.dram_tensor("sg_scr", [NCH, 128, 16, 128], F32, kind=dbg).ap()
    outs_scr = nc.dram_tensor("outs_scr", [NCH, 128, 16, 128], BF16, kind=dbg).ap()
    xn_scr = nc.dram_tensor("xn_scr", [16, 128, 2048], BF16, kind=dbg).ap()
    rows_hbm = nc.dram_tensor("rows_hbm", [8, 2048], F32, kind=dbg).ap()
    we_hbm = nc.dram_tensor("we_hbm", [8, 2048], F32, kind=dbg).ap()

    es = ExitStack()
    sbuf = lambda n, s, d: es.enter_context(nc.sbuf_tensor(n, s, d))
    psum = lambda n, s, d=F32: es.enter_context(nc.psum_tensor(n, s, d))
    sem = lambda n: es.enter_context(nc.semaphore(n))

    with es:
        A = sbuf("poolA", [128, 16 * 2048], BF16)
        B = sbuf("poolB", [128, 16 * 2048], BF16)
        Cp = sbuf("poolC", [128, 16 * 2048], BF16)
        Af = A[:, :].bitcast(F32)
        Bf = B[:, :].bitcast(F32)
        Cf = Cp[:, :].bitcast(F32)
        At = lambda i: A[:, i * 2048:(i + 1) * 2048]
        Bt = lambda i: B[:, i * 2048:(i + 1) * 2048]
        Ct = lambda i: Cp[:, i * 2048:(i + 1) * 2048]

        lns_t = sbuf("lns_t", [128, 16], F32)
        lnb_t = sbuf("lnb_t", [128, 16], F32)
        bwn_t = sbuf("bwn_t", [128, 16], F32)
        iop = sbuf("iop", [128, 1], F32)
        onesc = sbuf("onesc", [128, 1], BF16)
        onesf = sbuf("onesf", [128, 1], F32)
        we_rep = sbuf("we_rep", [128, 2048], F32)
        sgl = [sbuf("sgl0", [128, 512], F32), sbuf("sgl1", [128, 512], F32)]
        decay16 = sbuf("decay16", [128, 16], F32)

        psb = [psum(f"ps{i}", [128, 512], F32) for i in range(7)]
        psT = psum("psT", [128, 1024], BF16)

        dX = sem("dX")
        dW = sem("dW")
        dS = sem("dS")
        mmS = sem("mmS")
        scS = sem("scS")
        evS = sem("evS")
        N = dict(dX=0, dW=0, dS=0, mm=0, sc=0, ev=0)

        v3 = lambda ap2, k, t: ap2.rearrange("p (k t) -> p k t", k=k, t=t)

        with nc.Block() as blk:
            @blk.sync
            def _(s):
                s.dma_start(lns_t[:, :], lns_h[:, :]).then_inc(dX, 16)
                s.dma_start(lnb_t[:, :], lnb_h[:, :]).then_inc(dX, 16)
                s.dma_start(bwn_t[:, :], bw_h[:, :]).then_inc(dX, 16)

            @blk.gpsimd
            def _(g):
                g.iota(iop[:, :], [[0, 1]], channel_multiplier=1,
                       allow_small_or_imprecise_dtypes=True)
                g.memset(onesc[:, :], 1.0)
                g.memset(onesf[:, :], 1.0)
        N['dX'] += 48
        with nc.Block() as blk:
            @blk.vector
            def _(v):
                v.wait_ge(dX, N['dX'])
                v.tensor_scalar_mul(bwn_t[:, :], bwn_t[:, :], -1.0)

        row_t = Cf[0:1, 4096:8192]
        tmq = Cf[0:1, 12288:16384]
        xbuf = [Bf[:, 0:2048], Bf[:, 2048:4096]]
        xbbf = [Ct(0), Ct(1)]
        xqbf = [Ct(2), Ct(3)]
        b = dict(N)
        with nc.Block() as blk:
            @blk.sync
            def _(s):
                for fc in range(16):
                    if fc >= 2:
                        s.wait_ge(scS, b['sc'] + 2 * (fc - 1))
                    s.dma_start(xbuf[fc % 2], x_h[fc * 128:(fc + 1) * 128, :]
                                ).then_inc(dX, 16)

            @blk.scalar
            def _(s):
                for fc in range(16):
                    s.wait_ge(dX, b['dX'] + 16 * (fc + 1))
                    if fc >= 2:
                        s.wait_ge(mmS, b['mm'] + fc - 1)
                    s.activation(xbbf[fc % 2], xbuf[fc % 2], AF.Copy
                                 ).then_inc(scS, 1)
                    s.activation(xqbf[fc % 2], xbuf[fc % 2], AF.Square
                                 ).then_inc(scS, 1)

            @blk.tensor
            def _(t):
                for fc in range(16):
                    t.wait_ge(scS, b['sc'] + 2 * (fc + 1))
                    for tt in range(4):
                        t.matmul(psb[tt][0:1, :], onesc[:, :],
                                 xbbf[fc % 2][:, tt * 512:(tt + 1) * 512],
                                 start=(fc == 0), stop=(fc == 15),
                                 tile_position=(0, 0))
                        mi = t.matmul(psb[tt][32:33, :], onesc[:, :],
                                      xqbf[fc % 2][:, tt * 512:(tt + 1) * 512],
                                      start=(fc == 0), stop=(fc == 15),
                                      tile_position=(0, 32))
                    mi.then_inc(mmS, 1)
        N['dX'] += 256
        N['sc'] += 32
        N['mm'] += 16

        mu_row = row_t[0:1, 0:2048]
        var_row = row_t[0:1, 2048:4096]
        tA = tmq[0:1, 0:2048]
        tB = tmq[0:1, 2048:4096]

        def stat_finish(lo_rows, hi_rows, hbm_lo, hbm_hi):
            with nc.Block() as blk:
                @blk.scalar
                def _(s):
                    for tt in range(4):
                        s.mul(mu_row[0:1, tt * 512:(tt + 1) * 512],
                              lo_rows[tt], 1.0 / C)
                        s.mul(var_row[0:1, tt * 512:(tt + 1) * 512],
                              hi_rows[tt], 1.0 / C)
            with nc.Block() as blk:
                @blk.scalar
                def _(s):
                    s.activation(tA, mu_row, AF.Square)
            with nc.Block() as blk:
                @blk.vector
                def _(v):
                    v.tensor_sub(var_row, var_row, tA)
                    v.tensor_scalar_add(var_row, var_row, EPS)
            with nc.Block() as blk:
                @blk.scalar
                def _(s):
                    s.activation(tA, var_row, AF.Sqrt)
            with nc.Block() as blk:
                @blk.vector
                def _(v):
                    v.reciprocal(tB, tA)
            with nc.Block() as blk:
                @blk.vector
                def _(v):
                    v.tensor_mul(tA, tB, tB)
                    v.tensor_mul(tA, tA, var_row)
                    v.tensor_scalar(tA, tA, -0.5, 1.5, ALU.mult, ALU.add)
            with nc.Block() as blk:
                @blk.vector
                def _(v):
                    v.tensor_mul(var_row, tB, tA)
            with nc.Block() as blk:
                @blk.sync
                def _(s):
                    s.dma_start(hbm_lo, mu_row).then_inc(dX, 16)
                    s.dma_start(hbm_hi, var_row).then_inc(dX, 16)
                    N['dX'] += 32
                    s.wait_ge(dX, N['dX'])

        stat_finish([psb[tt][0:1, :] for tt in range(4)],
                    [psb[tt][32:33, :] for tt in range(4)],
                    rows_hbm[0:1, :], rows_hbm[1:2, :])

        mu_rep = Cf[:, 8192:10240]
        inv_rep = Cf[:, 10240:12288]
        with nc.Block() as blk:
            @blk.sync
            def _(s):
                s.dma_start(mu_rep, rows_hbm[0:1, :].broadcast_to([128, 2048])
                            ).then_inc(dX, 16)
                s.dma_start(inv_rep, rows_hbm[1:2, :].broadcast_to([128, 2048])
                            ).then_inc(dX, 16)
                N['dX'] += 32
                s.wait_ge(dX, N['dX'])

        gwt8 = Bf[:, 15040:15168]
        gb128 = Bf[:, 15184:15312]
        idf = Bf[:, 15312:15440]
        iof = Bf[:, 15440:15952]
        sc_fm = Bf[0:8, 12288:14336]
        sc_tm = Bf[:, 14336:14464]
        bsc = Bf[:, 14464:14592]
        m1c = Bf[:, 14592:14608]
        mskt = Bf[:, 14608:14736]
        m2c = Bf[:, 14736:14752]
        wselt = Bf[:, 14752:14880]
        den16 = Bf[:, 14880:14896]
        rden = Bf[:, 14896:14912]
        we_tm = Bf[:, 14912:15040]

        with nc.Block() as blk:
            @blk.gpsimd
            def _(g):
                g.iota(iof, [[0, 4], [1, 128]], channel_multiplier=0,
                       allow_small_or_imprecise_dtypes=True)
        with nc.Block() as blk:
            @blk.sync
            def _(s):
                s.dma_start(v3(gwt8, 16, 8),
                            gate_h.transpose([1, 0, 2])).then_inc(dX, 16)
                s.dma_start(v3(gb128, 16, 8),
                            gb_h.unsqueeze(1).broadcast_to([128, 16, 8])
                            ).then_inc(dX, 16)
        N['dX'] += 32
        with nc.Block() as blk:
            @blk.vector
            def _(v):
                v.wait_ge(dX, N['dX'])
                v.tensor_scalar(idf, iof[:, 0:128], iop[:, :], None,
                                ALU.is_equal, ALU.bypass)

        xnf = [Bf[:, 4096:6144], Bf[:, 6144:8192]]
        b = dict(N)
        with nc.Block() as blk:
            @blk.sync
            def _(s):
                for fc in range(16):
                    if fc >= 2:
                        s.wait_ge(evS, b['ev'] + 2 * (fc - 1))
                    s.dma_start(xbuf[fc % 2], x_h[fc * 128:(fc + 1) * 128, :]
                                ).then_inc(dX, 16)

            @blk.vector
            def _(v):
                for fc in range(16):
                    v.wait_ge(dX, b['dX'] + 16 * (fc + 1))
                    if fc >= 2:
                        v.wait_ge(scS, b['sc'] + fc - 1)
                        v.wait_ge(mmS, b['mm'] + fc - 1)
                    v.tensor_sub(xnf[fc % 2], xbuf[fc % 2], mu_rep)
                    v.tensor_mul(xnf[fc % 2], xnf[fc % 2], inv_rep
                                 ).then_inc(evS, 1)
                    v.tensor_copy(At(fc), xnf[fc % 2]).then_inc(evS, 1)

            @blk.scalar
            def _(s):
                for fc in range(16):
                    s.wait_ge(evS, b['ev'] + 2 * fc + 2)
                    s.activation(xqbf[fc % 2], xnf[fc % 2], AF.Square
                                 ).then_inc(scS, 1)
                    s.dma_start(xn_scr[fc], At(fc)).then_inc(dS, 16)

            @blk.tensor
            def _(t):
                for fc in range(16):
                    t.wait_ge(evS, b['ev'] + 2 * fc + 1)
                    t.wait_ge(scS, b['sc'] + fc + 1)
                    for tt in range(4):
                        t.matmul(psb[tt][0:8, :],
                                 gwt8[:, fc * 8:(fc + 1) * 8],
                                 xnf[fc % 2][:, tt * 512:(tt + 1) * 512],
                                 start=(fc == 0), stop=(fc == 15),
                                 tile_position=(0, 0))
                        t.matmul(psb[tt][64:65, :], onesf[:, :],
                                 xnf[fc % 2][:, tt * 512:(tt + 1) * 512],
                                 start=(fc == 0), stop=(fc == 15),
                                 tile_position=(0, 64))
                        mi = t.matmul(psb[tt][32:33, :], onesc[:, :],
                                      xqbf[fc % 2][:, tt * 512:(tt + 1) * 512],
                                      start=(fc == 0), stop=(fc == 15),
                                      tile_position=(0, 32))
                    mi.then_inc(mmS, 1)
        N['dX'] += 256
        N['ev'] += 32
        N['sc'] += 16
        N['dS'] += 256
        N['mm'] += 16

        with nc.Block() as blk:
            @blk.scalar
            def _(s):
                for tt in range(4):
                    s.activation(sc_fm[:, tt * 512:(tt + 1) * 512],
                                 psb[tt][0:8, :], AF.Sigmoid)
        stat_finish([psb[tt][64:65, :] for tt in range(4)],
                    [psb[tt][32:33, :] for tt in range(4)],
                    rows_hbm[2:3, :], rows_hbm[3:4, :])

        with nc.Block() as blk:
            @blk.tensor
            def _(t):
                for i in range(16):
                    t.matmul(psb[4][:, i * 8:(i + 1) * 8],
                             sc_fm[:, i * 128:(i + 1) * 128],
                             idf[0:8, 0:8], start=True, stop=True)
        with nc.Block() as blk:
            @blk.vector
            def _(v):
                v.tensor_copy(sc_tm, psb[4][:, 0:128])
        with nc.Block() as blk:
            @blk.vector
            def _(v):
                v.tensor_add(bsc, sc_tm, gb128)
        with nc.Block() as blk:
            @blk.vector
            def _(v):
                v.tensor_reduce(m1c, bsc.rearrange("p (i e) -> p i e", i=16),
                                AX.X, ALU.max)
        with nc.Block() as blk:
            @blk.vector
            def _(v):
                for i in range(16):
                    v.tensor_scalar(mskt[:, i * 8:(i + 1) * 8],
                                    bsc[:, i * 8:(i + 1) * 8],
                                    m1c[:, i:i + 1], None,
                                    ALU.is_equal, ALU.bypass)
        with nc.Block() as blk:
            @blk.vector
            def _(v):
                v.scalar_tensor_tensor(mskt, mskt, -1e30, bsc,
                                       ALU.mult, ALU.add)
        with nc.Block() as blk:
            @blk.vector
            def _(v):
                v.tensor_reduce(m2c, mskt.rearrange("p (i e) -> p i e", i=16),
                                AX.X, ALU.max)
        with nc.Block() as blk:
            @blk.vector
            def _(v):
                for i in range(16):
                    v.tensor_scalar(wselt[:, i * 8:(i + 1) * 8],
                                    bsc[:, i * 8:(i + 1) * 8],
                                    m2c[:, i:i + 1], None,
                                    ALU.is_ge, ALU.bypass)
        with nc.Block() as blk:
            @blk.vector
            def _(v):
                v.tensor_mul(wselt, wselt, sc_tm)
        with nc.Block() as blk:
            @blk.vector
            def _(v):
                v.tensor_reduce(den16, wselt.rearrange("p (i e) -> p i e", i=16),
                                AX.X, ALU.add)
        with nc.Block() as blk:
            @blk.vector
            def _(v):
                v.reciprocal(rden, den16)
        with nc.Block() as blk:
            @blk.vector
            def _(v):
                for i in range(16):
                    v.tensor_scalar(we_tm[:, i * 8:(i + 1) * 8],
                                    wselt[:, i * 8:(i + 1) * 8],
                                    rden[:, i:i + 1], None,
                                    ALU.mult, ALU.bypass)
        we_fmB = Bf[0:8, 12288:14336]
        for gix in range(4):
            with nc.Block() as blk:
                @blk.tensor
                def _(t):
                    for j in range(4):
                        i = gix * 4 + j
                        t.matmul(psb[5][0:8, j * 128:(j + 1) * 128],
                                 we_tm[:, i * 8:(i + 1) * 8],
                                 idf[:, 0:128], start=True, stop=True)
            with nc.Block() as blk:
                @blk.vector
                def _(v):
                    v.tensor_copy(we_fmB[:, gix * 512:(gix + 1) * 512],
                                  psb[5][0:8, :])
        with nc.Block() as blk:
            @blk.sync
            def _(s):
                s.dma_start(we_hbm[:, :], we_fmB).then_inc(dX, 16)
                N['dX'] += 16
                s.wait_ge(dX, N['dX'])

        mu2_rep = Cf[:, 8192:10240]
        inv2_rep = Cf[:, 10240:12288]
        with nc.Block() as blk:
            @blk.sync
            def _(s):
                s.dma_start(mu2_rep, rows_hbm[2:3, :].broadcast_to([128, 2048])
                            ).then_inc(dX, 16)
                s.dma_start(inv2_rep, rows_hbm[3:4, :].broadcast_to([128, 2048])
                            ).then_inc(dX, 16)
                N['dX'] += 32
                s.wait_ge(dX, N['dX'])
        with nc.Block() as blk:
            @blk.vector
            def _(v):
                for fc in range(16):
                    v.tensor_sub(we_rep[:, :], At(fc), mu2_rep)
                    v.tensor_mul(we_rep[:, :], we_rep[:, :], inv2_rep)
                    v.tensor_scalar(Bt(fc), we_rep[:, :],
                                    lns_t[:, fc:fc + 1], lnb_t[:, fc:fc + 1],
                                    ALU.mult, ALU.add)

        LPb = Af[:, 0:2048]
        mask4 = A[:, 4096:4608]
        identb = A[:, 4608:4736]
        ones_f = Af[:, 2368:2496]
        S_all = Af[:, 3072:4096]
        la_lo = Af[:, 4096:5120]
        la_hi = Af[:, 7168:8192]
        S_bf = A[:, 5 * 2048:5 * 2048 + 1024]
        outs_sb = At(6)
        smbf = [A[:, 10 * 2048:10 * 2048 + 512], A[:, 11 * 2048:11 * 2048 + 512]]
        khall = At(12)
        vtm = At(13)
        khtm = At(14)
        rtiall = At(15)

        _FB = [8 * 1024, 9 * 1024, 10496]
        _KB = [8 * 2048 + 1408, 9 * 2048 + 1408, 23040]

        def f_ekt(sl, j):
            off = [0, 32, 96, 192][j]
            return Af[:, _FB[sl] + off: _FB[sl] + off + 32 * (j + 1)]

        def f_er(sl):
            return Af[:, _FB[sl] + 320:_FB[sl] + 448]

        def f_eri(sl):
            return Af[:, _FB[sl] + 448:_FB[sl] + 576]

        def f_ekh(sl):
            return Af[:, _FB[sl] + 576:_FB[sl] + 704]

        def f_ktj(sl, j):
            off = [0, 32, 96, 192][j]
            return A[:, _KB[sl] + off: _KB[sl] + off + 32 * (j + 1)]

        def f_rt(sl):
            return A[:, _KB[sl] + 320: _KB[sl] + 448]

        kcb = [Ct(4), Ct(9)]
        rcb = [Ct(5), Ct(10)]
        vcb = [Ct(6), Ct(11)]
        sgb = Cf[:, 7 * 1024:9 * 1024]
        wbuf = [Ct(0), Ct(1)]
        ktile = [Cp[:, 2 * 2048:2 * 2048 + 512],
                 Cp[:, 2 * 2048 + 512:2 * 2048 + 1024]]
        sgtile = [Cf[:, 3 * 1024:3 * 1024 + 512],
                  Cf[:, 3 * 1024 + 512:3 * 1024 + 1024]]
        t1sl = lambda kc: Cp[:, 12 * 2048 + kc * 512:12 * 2048 + (kc + 1) * 512]
        la_sl = lambda ft: (la_lo if ft < 8 else la_hi)[
            :, (ft % 8) * 128:(ft % 8) * 128 + 128]

        mmQ = sem("mmQ")
        scQ = sem("scQ")
        evQ = sem("evQ")
        dQ = sem("dQ")

        EVC = 225
        evSbf = lambda c: EVC * c + 1
        evScan = lambda c, ft: EVC * c + 2 + ft
        _FBASE = [17, 65, 112, 159]
        evVtm = lambda c, r: EVC * c + 62 + r
        evFact = lambda c, ft: (EVC * c + _FBASE[ft // 4]
                                + 11 * (ft % 4 + 1))
        evMask = lambda c, q: EVC * c + ((64 + q) if q < 2 else
                                         (65 + 47 * (q // 2 - 1) + 46 + q % 2))
        evOuts = lambda c, oo: EVC * c + (207 if oo == 3 else
                                          65 + 47 * oo + 45)
        evKhtm = lambda c, r: EVC * c + 208 + r
        evUpdD = lambda c, q: EVC * c + 209 + 2 * (q + 1)
        scLa = lambda c: 99 * c + 2
        scExps = lambda c, ft: 99 * c + 2 + 6 * (ft + 1)
        scDecay = lambda c: 99 * c + 99
        mmVtm = lambda c, r: 24 * c + [2, 4][r]
        mmScore = lambda c, q: 24 * c + [1, 3, 5, 6, 8, 10, 11, 13][q]
        mmOuts = lambda c, oo: 24 * c + [7, 9, 12, 14][oo]
        mmKh = lambda c, r: 24 * c + 15 + r
        mmUpd = lambda c, q: 24 * c + 17 + q
        dQd = lambda c: 64 * (c + 1)

        with nc.Block() as blk:
            @blk.gpsimd
            def _(g):
                g.iota(Af[:, 2560:3072], [[0, 4], [1, 128]],
                       channel_multiplier=0,
                       allow_small_or_imprecise_dtypes=True)

            @blk.vector
            def _(v):
                v.memset(S_all, 0.0)
                v.memset(ones_f, 1.0)
                v.memset(psb[2][:, :], 0.0)
                v.memset(psb[3][:, :], 0.0)
        with nc.Block() as blk:
            @blk.vector
            def _(v):
                v.tensor_scalar(mask4, Af[:, 2560:3072], iop[:, :], None,
                                ALU.is_ge, ALU.bypass)
                v.tensor_scalar(identb, Af[:, 2560:2688], iop[:, :], None,
                                ALU.is_equal, ALU.bypass)

        def T_packets(ch):
            pk = []

            def vtmr(r):
                def f(t):
                    t.wait_ge(dQ, dQd(ch))
                    if ch >= 1:
                        t.wait_ge(evQ, EVC * ch)
                    if r == 1:
                        t.wait_ge(evQ, evVtm(ch, 0))

                    for i in range(8):
                        ft = r * 8 + i
                        mi = t.transpose(psT[:, i * 128:(i + 1) * 128],
                                         vcb[ch % 2][:, ft * 128:(ft + 1) * 128],
                                         identb)
                    mi.then_inc(mmQ, 1)
                return f

            def score(q):
                def f(t):
                    t.wait_ge(evQ, evFact(ch, 2 * q + 1))
                    if q >= 2:
                        t.wait_ge(evQ, evMask(ch, q - 2))
                    elif ch >= 1:
                        t.wait_ge(evQ, evMask(ch - 1, q + 6))
                    for hp in range(4):
                        hh = hp % 2
                        ftq = 2 * q + hp // 2
                        sl = ftq % 3
                        for j in range(4):
                            w = 32 * (j + 1)
                            mi = t.matmul(
                                psb[2 + q % 2][0:w, hp * 128 + 32 * j:
                                               hp * 128 + 32 * (j + 1)],
                                f_ktj(sl, j)[hh * 64:(hh + 1) * 64, :],
                                f_rt(sl)[hh * 64:(hh + 1) * 64,
                                         32 * j:32 * (j + 1)],
                                start=True, stop=True,
                                tile_position=(hh * 64, 0))
                    mi.then_inc(mmQ, 1)
                return f

            def outsg(oo):
                def f(t):
                    if oo >= 2:
                        t.wait_ge(evQ, evOuts(ch, oo - 2))
                    elif ch >= 1:
                        t.wait_ge(evQ, evOuts(ch - 1, oo + 2))
                        t.wait_ge(evQ, evUpdD(ch - 1, 6 + oo))
                    mi = None
                    for q in (2 * oo, 2 * oo + 1):
                        t.wait_ge(evQ, evMask(ch, q))
                        t.wait_ge(evQ, evVtm(ch, 1))
                        for hp in range(4):
                            head = 4 * q + hp
                            hh = hp % 2
                            ftq = 2 * q + hp // 2
                            slot = (q % 2) * 2 + hp // 2
                            t.matmul(
                                psb[4 + oo % 2][hh * 64:(hh + 1) * 64,
                                                slot * 128:(slot + 1) * 128],
                                vtm[:, head * 64:(head + 1) * 64],
                                smbf[q % 2][:, hp * 128:(hp + 1) * 128],
                                start=True, stop=False,
                                tile_position=(0, hh * 64))
                            mi = t.matmul(
                                psb[4 + oo % 2][hh * 64:(hh + 1) * 64,
                                                slot * 128:(slot + 1) * 128],
                                S_bf[hh * 64:(hh + 1) * 64,
                                     ftq * 64:(ftq + 1) * 64],
                                rtiall[hh * 64:(hh + 1) * 64,
                                       ftq * 128:(ftq + 1) * 128],
                                start=False, stop=True,
                                tile_position=(hh * 64, hh * 64))
                    mi.then_inc(mmQ, 1)
                return f

            def khr(r):
                def f(t):
                    t.wait_ge(evQ, evFact(ch, r * 8 + 7))
                    t.wait_ge(evQ, evVtm(ch, 1) if r == 0 else evKhtm(ch, 0))
                    for i in range(8):
                        ft = r * 8 + i
                        mi = t.transpose(psT[:, i * 128:(i + 1) * 128],
                                         khall[:, ft * 128:(ft + 1) * 128],
                                         identb)
                    mi.then_inc(mmQ, 1)
                return f

            def upd(q):
                def f(t):
                    t.wait_ge(evQ, evKhtm(ch, q // 4))
                    if q >= 2:
                        t.wait_ge(evQ, evUpdD(ch, q - 2))
                    else:
                        t.wait_ge(evQ, evOuts(ch, 2 + q))
                    for hp in range(4):
                        head = 4 * q + hp
                        hh = hp % 2
                        pp = hp // 2
                        mi = t.matmul(
                            psb[4 + q % 2][hh * 64:(hh + 1) * 64,
                                           (q // 2) * 128 + pp * 64:
                                           (q // 2) * 128 + pp * 64 + 64],
                            khtm[:, head * 64:(head + 1) * 64],
                            vtm[:, head * 64:(head + 1) * 64],
                            start=True, stop=True,
                            tile_position=(0, hh * 64))
                    mi.then_inc(mmQ, 1)
                return f

            pk.append(score(0))
            pk.append(vtmr(0))
            pk.append(score(1))
            pk.append(vtmr(1))
            pk.append(score(2))
            pk.append(score(3))
            pk.append(outsg(0))
            pk.append(score(4))
            pk.append(outsg(1))
            pk.append(score(5))
            pk.append(score(6))
            pk.append(outsg(2))
            pk.append(score(7))
            pk.append(outsg(3))
            pk.append(khr(0))
            pk.append(khr(1))
            for q in range(8):
                pk.append(upd(q))
            return pk

        SD = {}
        Dms = {}
        dsc = [N['dS']]

        def S_packets(ch):
            pk = []

            def la(half):
                def f(s):
                    if half == 0:
                        s.wait_ge(dQ, dQd(ch))
                        if ch >= 1:
                            s.wait_ge(evQ, evScan(ch - 1, 15))
                    dst = la_lo if half == 0 else la_hi
                    s.activation(dst, sgb[:, half * 1024:(half + 1) * 1024],
                                 AF.Ln).then_inc(scQ, 1)
                return f

            def exps(ft):
                def f(s):
                    s.wait_ge(evQ, evScan(ch, ft))
                    if ft >= 3:
                        s.wait_ge(evQ, evFact(ch, ft - 3))
                    lp = LPb[:, ft * 128:(ft + 1) * 128]
                    sl = ft % 3
                    for j in range(4):
                        w = 32 * (j + 1)
                        bias = (lp[:, 32 * j - 1:32 * j] if j > 0 else 0.0)
                        s.activation(f_ekt(sl, j), lp[:, 0:w], AF.Exp,
                                     scale=-1.0, bias=bias).then_inc(scQ, 1)
                    s.activation(f_eri(sl), lp, AF.Exp).then_inc(scQ, 1)
                    s.activation(f_ekh(sl), lp, AF.Exp, scale=-1.0,
                                 bias=lp[:, 127:128]).then_inc(scQ, 1)
                return f

            def decay(s):
                if ch >= 1:
                    s.wait_ge(evQ, evUpdD(ch - 1, 7))
                s.activation(decay16[:, :], LPb[:, 127:2048:128], AF.Exp
                             ).then_inc(scQ, 1)

            def odma(s):
                s.wait_ge(evQ, evOuts(ch, 3))
                s.dma_start(outs_scr[ch].rearrange("p f t -> p (f t)"),
                            outs_sb).then_inc(dS, 16)
                dsc[0] += 16
                SD[ch] = dsc[0]

            pk.append(la(0))
            pk.append(la(1))
            for ft in range(16):
                pk.append(exps(ft))
            pk.append(decay)
            pk.append(odma)
            return pk

        def LD_emit(s, ch):
            if ch >= 2:
                s.wait_ge(evQ, EVC * (ch - 1))
                s.wait_ge(mmQ, 24 * (ch - 1))
            if ch >= 1:
                s.wait_ge(scQ, scLa(ch - 1))
            s.wait_ge(dS, Dms[ch // 4])
            s.dma_start(kcb[ch % 2], k_scr[ch].rearrange("p f t -> p (f t)")
                        ).then_inc(dQ, 16)
            s.dma_start(rcb[ch % 2], r_scr[ch].rearrange("p f t -> p (f t)")
                        ).then_inc(dQ, 16)
            s.dma_start(vcb[ch % 2], v_scr[ch].rearrange("p f t -> p (f t)")
                        ).then_inc(dQ, 16)
            s.dma_start(sgb, sg_scr[ch].rearrange("p f t -> p (f t)")
                        ).then_inc(dQ, 16)

        def V_chunk(v, ch):
            if ch >= 1:
                v.wait_ge(mmQ, mmOuts(ch - 1, 3))
            v.tensor_copy(S_bf, S_all).then_inc(evQ, 1)
            v.wait_ge(scQ, scLa(ch))
            for ft in range(16):
                v.tensor_tensor_scan(LPb[:, ft * 128:(ft + 1) * 128], ones_f,
                                     la_sl(ft), 0.0, ALU.mult, ALU.add
                                     ).then_inc(evQ, 1)
            def v_oevac(oo):
                v.wait_ge(mmQ, mmOuts(ch, oo))
                if oo == 0 and ch >= 1:
                    v.wait_ge(dS, SD[ch - 1])
                v.tensor_copy(outs_sb[:, oo * 512:(oo + 1) * 512],
                              psb[4 + oo % 2][:, :]).then_inc(evQ, 1)

            for oo in range(4):
                for i in range(4):
                    ft = 4 * oo + i
                    v.wait_ge(scQ, scExps(ch, ft))
                    if ft >= 3:
                        v.wait_ge(mmQ, mmScore(ch, (ft - 3) // 2))
                    elif ch >= 1:
                        v.wait_ge(mmQ, 24 * (ch - 1) + 16)
                    sl = ft % 3
                    fsl = ft * 128
                    for j in range(4):
                        v.reciprocal(f_er(sl)[:, 32 * j:32 * (j + 1)],
                                     f_ekt(sl, j)[:, 32 * j:32 * (j + 1)]
                                     ).then_inc(evQ, 1)
                    v.tensor_tensor(f_rt(sl), rcb[ch % 2][:, fsl:fsl + 128],
                                    f_er(sl), ALU.mult).then_inc(evQ, 1)
                    for j in range(4):
                        v.tensor_tensor(f_ktj(sl, j),
                                        kcb[ch % 2][:, fsl:fsl + 32 * (j + 1)],
                                        f_ekt(sl, j), ALU.mult).then_inc(evQ, 1)
                    v.tensor_tensor(rtiall[:, fsl:fsl + 128],
                                    rcb[ch % 2][:, fsl:fsl + 128],
                                    f_eri(sl), ALU.mult).then_inc(evQ, 1)
                    v.tensor_tensor(khall[:, fsl:fsl + 128],
                                    kcb[ch % 2][:, fsl:fsl + 128],
                                    f_ekh(sl), ALU.mult).then_inc(evQ, 1)
                if oo == 0:
                    for r in range(2):
                        v.wait_ge(mmQ, mmVtm(ch, r))
                        v.tensor_copy(vtm[:, r * 1024:(r + 1) * 1024],
                                      psT[:, :]).then_inc(evQ, 1)
                else:
                    v_oevac(oo - 1)
                for q in (2 * oo, 2 * oo + 1):
                    v.wait_ge(mmQ, mmScore(ch, q))
                    if q >= 2:
                        v.wait_ge(mmQ, mmOuts(ch, (q - 2) // 2))
                    elif ch >= 1:
                        v.wait_ge(mmQ, mmOuts(ch - 1, (q + 6) // 2))
                    v.tensor_tensor(smbf[q % 2], psb[2 + q % 2][:, :],
                                    mask4, ALU.mult).then_inc(evQ, 1)
            v_oevac(3)
            for r in range(2):
                v.wait_ge(mmQ, mmKh(ch, r))
                v.tensor_copy(khtm[:, r * 1024:(r + 1) * 1024], psT[:, :]
                              ).then_inc(evQ, 1)
            v.wait_ge(scQ, scDecay(ch))
            for q in range(8):
                v.wait_ge(mmQ, mmUpd(ch, q))
                for pp in range(2):
                    ftq = 2 * q + pp
                    v.scalar_tensor_tensor(
                        S_all[:, ftq * 64:(ftq + 1) * 64],
                        S_all[:, ftq * 64:(ftq + 1) * 64],
                        decay16[:, ftq:ftq + 1],
                        psb[4 + q % 2][:, (q // 2) * 128 + pp * 64:
                                       (q // 2) * 128 + pp * 64 + 64],
                        ALU.mult, ALU.add).then_inc(evQ, 1)

        b4 = dict(N)
        Ws = [(wx_h, 't1', None), (ww_h, 'sg', sg_scr), (wk_h, 'copy', k_scr),
              (wv_h, 'copy', v_scr), (wr_h, 'sig', r_scr)]
        tile_ms = {}

        with nc.Block() as blk:
            @blk.gpsimd
            def _(g):
                for gg in range(4):
                    for wi in range(5):
                        w_src = Ws[wi][0]
                        for fc in range(16):
                            sidx = gg * 80 + wi * 16 + fc
                            if sidx >= 2:
                                g.wait_ge(mmS, b4['mm'] + sidx - 1)
                            g.dma_start(v3(wbuf[sidx % 2], 16, 128), w_src[fc]
                                        ).then_inc(dW, 16)

            @blk.scalar
            def _(s):
                spk = []
                sdone = [0]

                def spk_slot(slot):
                    while (sdone[0] < len(spk)
                           and spk[sdone[0]][0] <= slot):
                        spk[sdone[0]][1](s)
                        sdone[0] += 1

                for gg in range(4):
                    spk[:] = []
                    sdone[0] = 0
                    SOFF = ([0, 1] + [2 + int(ft * 0.85) for ft in range(16)]
                            + [16, 24])
                    if gg >= 1:
                        for c in range(4 * (gg - 1), 4 * gg):
                            base = 18 * (c % 4)
                            pkts = S_packets(c)
                            spk.extend((base + SOFF[j], f)
                                       for j, f in enumerate(pkts))
                    for wi in range(5):
                        kind = Ws[wi][1]
                        scr_dst = Ws[wi][2]
                        for fc in range(16):
                            sidx = gg * 80 + wi * 16 + fc
                            slot = wi * 16 + fc
                            s.wait_ge(mmS, b4['mm'] + sidx + 1)
                            ps = psb[[0, 1, 6][sidx % 3]][:, :]
                            if kind == 't1':
                                s.activation(t1sl(fc), ps, AF.Copy
                                             ).then_inc(scS, 1)
                            else:
                                tl = (ktile if kind != 'sg' else sgtile)
                                tid = (kind != 'sg', sidx % 2)
                                if tid in tile_ms:
                                    s.wait_ge(dS, tile_ms[tid])
                                if kind == 'copy':
                                    s.activation(tl[sidx % 2], ps, AF.Copy
                                                 ).then_inc(scS, 1)
                                elif kind == 'sig':
                                    s.activation(tl[sidx % 2], ps, AF.Sigmoid
                                                 ).then_inc(scS, 1)
                                else:
                                    s.activation(tl[sidx % 2], ps, AF.Sigmoid,
                                                 scale=-1.0,
                                                 bias=bwn_t[:, fc:fc + 1]
                                                 ).then_inc(scS, 1)
                                s.wait_ge(scS, b4['sc'] + sidx + 1)
                                s.dma_start(
                                    scr_dst[4 * gg:4 * gg + 4, :, fc, :]
                                    .transpose([1, 0, 2]),
                                    v3(tl[sidx % 2], 4, 128)).then_inc(dS, 16)
                                dsc[0] += 16
                                tile_ms[tid] = dsc[0]
                            spk_slot(slot)
                    spk_slot(10 ** 9)
                    Dms[gg] = dsc[0]
                spk[:] = []
                sdone[0] = 0
                for c in range(12, 16):
                    spk.extend((0, f) for f in S_packets(c))
                spk_slot(10 ** 9)

            @blk.sync
            def _(s):
                for ch in range(NCH):
                    LD_emit(s, ch)

            @blk.tensor
            def _(t):
                tpk = []
                tdone = [0]

                def tpk_slot(gslot):
                    while (tdone[0] < len(tpk)
                           and tpk[tdone[0]][0] <= gslot):
                        tpk[tdone[0]][1](t)
                        tdone[0] += 1

                TOFF = [6, 7, 8, 9, 10, 11, 12, 13, 15, 15, 16, 18,
                        18, 21, 21, 21, 23, 23, 23, 23, 23, 23, 23, 23]
                for gg in range(4):
                    if gg >= 1:
                        for c in range(4 * (gg - 1), 4 * gg):
                            pkts = T_packets(c)
                            base = 80 * gg + 18 * (c % 4)
                            tpk.extend((base + TOFF[j], f)
                                       for j, f in enumerate(pkts))
                    for wi in range(5):
                        kind = Ws[wi][1]
                        for fc in range(16):
                            sidx = gg * 80 + wi * 16 + fc
                            slot = wi * 16 + fc
                            t.wait_ge(dW, b4['dW'] + 16 * (sidx + 1))
                            if sidx >= 3:
                                t.wait_ge(scS, b4['sc'] + sidx - 2)
                            if kind == 'sg':
                                t.wait_ge(scS, b4['sc'] + gg * 80 + 16)
                            for kc in range(16):
                                rhs = (t1sl(kc) if kind == 'sg'
                                       else Bt(kc)[:, gg * 512:(gg + 1) * 512])
                                mi = t.matmul(
                                    psb[[0, 1, 6][sidx % 3]][:, :],
                                    wbuf[sidx % 2][:, kc * 128:(kc + 1) * 128],
                                    rhs, start=(kc == 0), stop=(kc == 15))
                            mi.then_inc(mmS, 1)
                            tpk_slot(sidx)
                for c in range(12, 16):
                    tpk.extend((0, f) for f in T_packets(c))
                while tdone[0] < len(tpk):
                    tpk[tdone[0]][1](t)
                    tdone[0] += 1

            @blk.vector
            def _(v):
                for ch in range(NCH):
                    V_chunk(v, ch)

        N['dW'] += 16 * 320
        N['mm'] += 320
        N['sc'] += 320
        N['dS'] = dsc[0]

        with nc.Block() as blk:
            @blk.sync
            def _(s):
                s.wait_ge(dS, N['dS'])
                for fc in range(16):
                    s.dma_start(At(fc), xn_scr[fc]).then_inc(dX, 16)
                N['dX'] += 256
                s.wait_ge(dX, N['dX'])

        gbuf = [Bt(11), Bt(12)]
        ubuf = [Bt(13), Bt(14)]
        dbuf = [B[:, 15 * 2048:15 * 2048 + 1408],
                B[:, 11 * 2048:11 * 2048 + 1408]]

        def ffn_expert(eidx, g_src, y_src, dn_src, routed, first):
            b = dict(N)
            with nc.Block() as blk:
                @blk.sync
                def _(s):
                    if routed:
                        s.dma_start(we_rep[:, :],
                                    we_hbm[eidx:eidx + 1, :]
                                    .broadcast_to([128, 2048])).then_inc(dX, 16)

                @blk.gpsimd
                def _(g):
                    for j in range(11):
                        if j >= 2:
                            g.wait_ge(mmS, b['mm'] + 8 * (j - 1))
                        g.dma_start(v3(gbuf[j % 2], 16, 128), g_src(j)
                                    ).then_inc(dW, 16)
                        g.dma_start(v3(ubuf[j % 2], 16, 128), y_src(j)
                                    ).then_inc(dW, 16)

                @blk.tensor
                def _(t):
                    for j in range(11):
                        t.wait_ge(dW, b['dW'] + 32 * (j + 1))
                        for tt in range(4):
                            gi = 4 * j + tt
                            if gi >= 2:
                                t.wait_ge(scS, b['sc'] + gi - 1)
                                t.wait_ge(evS, b['ev'] + gi - 1)
                            for kc in range(16):
                                mi = t.matmul(
                                    psb[tt % 2][:, :],
                                    gbuf[j % 2][:, kc * 128:(kc + 1) * 128],
                                    At(kc)[:, tt * 512:(tt + 1) * 512],
                                    start=(kc == 0), stop=(kc == 15))
                            mi.then_inc(mmS, 1)
                            for kc in range(16):
                                mi = t.matmul(
                                    psb[2 + tt % 2][:, :],
                                    ubuf[j % 2][:, kc * 128:(kc + 1) * 128],
                                    At(kc)[:, tt * 512:(tt + 1) * 512],
                                    start=(kc == 0), stop=(kc == 15))
                            mi.then_inc(mmS, 1)

                @blk.scalar
                def _(s):
                    for j in range(11):
                        for tt in range(4):
                            gi = 4 * j + tt
                            s.wait_ge(mmS, b['mm'] + 2 * gi + 1)
                            s.activation(sgl[gi % 2][:, :],
                                         psb[tt % 2][:, :],
                                         AF.Silu).then_inc(scS, 1)

                @blk.vector
                def _(v):
                    if routed:
                        v.wait_ge(dX, b['dX'] + 16)
                    for j in range(11):
                        for tt in range(4):
                            gi = 4 * j + tt
                            v.wait_ge(mmS, b['mm'] + 2 * gi + 2)
                            v.wait_ge(scS, b['sc'] + gi + 1)
                            pu = psb[2 + tt % 2][:, :]
                            if routed:
                                v.tensor_mul(pu, pu,
                                             we_rep[:, tt * 512:(tt + 1) * 512])
                            v.tensor_tensor(
                                B[:, j * 2048 + tt * 512:
                                  j * 2048 + (tt + 1) * 512],
                                sgl[gi % 2][:, :], pu, ALU.mult
                            ).then_inc(evS, 1)
            N['dW'] += 32 * 11
            N['mm'] += 88
            N['sc'] += 44
            N['ev'] += 44
            if routed:
                N['dX'] += 16

            b = dict(N)
            with nc.Block() as blk:
                @blk.gpsimd
                def _(g):
                    for fc in range(16):
                        if fc >= 2:
                            g.wait_ge(mmS, b['mm'] + 4 * (fc - 1))
                        g.dma_start(v3(dbuf[fc % 2], 11, 128), dn_src(fc)
                                    ).then_inc(dW, 16)

                @blk.tensor
                def _(t):
                    for fc in range(16):
                        t.wait_ge(dW, b['dW'] + 16 * (fc + 1))
                        for tt in range(4):
                            gi = 4 * fc + tt
                            if gi >= 2:
                                t.wait_ge(evS, b['ev'] + gi - 1)
                            for kc in range(11):
                                mi = t.matmul(
                                    psb[4 + tt % 2][:, :],
                                    dbuf[fc % 2][:, kc * 128:(kc + 1) * 128],
                                    B[:, kc * 2048 + tt * 512:
                                      kc * 2048 + (tt + 1) * 512],
                                    start=(kc == 0), stop=(kc == 10))
                            mi.then_inc(mmS, 1)

                @blk.vector
                def _(v):
                    for fc in range(16):
                        for tt in range(4):
                            gi = 4 * fc + tt
                            v.wait_ge(mmS, b['mm'] + gi + 1)
                            dst = Ct(fc)[:, tt * 512:(tt + 1) * 512]
                            if first:
                                v.tensor_copy(dst, psb[4 + tt % 2][:, :]
                                              ).then_inc(evS, 1)
                            else:
                                v.tensor_tensor(dst, psb[4 + tt % 2][:, :],
                                                dst, ALU.add).then_inc(evS, 1)
            N['dW'] += 256
            N['mm'] += 64
            N['ev'] += 64

        for e in range(8):
            ffn_expert(e,
                       g_src=lambda j, e=e: upw_h[e, j],
                       y_src=lambda j, e=e: upw_h[e, 11 + j],
                       dn_src=lambda fc, e=e: dnw_h[e, fc],
                       routed=True, first=(e == 0))
        ffn_expert(8,
                   g_src=lambda j: gws_h[22 + j],
                   y_src=lambda j: gws_h[j],
                   dn_src=lambda fc: dws_h[fc, :, 0:11, :],
                   routed=False, first=False)
        ffn_expert(9,
                   g_src=lambda j: gws_h[33 + j],
                   y_src=lambda j: gws_h[11 + j],
                   dn_src=lambda fc: dws_h[fc, :, 11:22, :],
                   routed=False, first=False)

        outs_g = [A[:, 8 * 2048:12 * 2048], A[:, 12 * 2048:16 * 2048]]
        xg = [Af[:, 0:512], Af[:, 512:1024]]
        otile = [Af[:, 1024:1536], Af[:, 1536:2048]]
        b = dict(N)
        with nc.Block() as blk:
            @blk.gpsimd
            def _(g):
                for fc in range(16):
                    g.dma_start(v3(Bt(fc), 16, 128), wo_h[fc]).then_inc(dW, 16)

            @blk.sync
            def _(s):
                s.wait_ge(dS, b['dS'])
                for gg in range(4):
                    if gg >= 2:
                        s.wait_ge(mmS, b['mm'] + 16 * (gg - 1))
                    for cc in range(4):
                        s.dma_start(
                            outs_g[gg % 2].rearrange(
                                "p (k c t) -> p k c t", k=16, c=4, t=128
                            )[:, :, cc, :],
                            outs_scr[4 * gg + cc]).then_inc(dX, 16)
                    for fc in range(16):
                        gi = 16 * gg + fc
                        if gi >= 2:
                            s.wait_ge(evS, b['ev'] + gi - 1)
                        s.dma_start(xg[gi % 2],
                                    x_h[fc * 128:(fc + 1) * 128,
                                        gg * 512:(gg + 1) * 512]
                                    ).then_inc(dX, 16)

            @blk.tensor
            def _(t):
                t.wait_ge(dW, b['dW'] + 256)
                for gg in range(4):
                    t.wait_ge(dX, b['dX'] + 16 * (20 * gg + 4))
                    for fc in range(16):
                        gi = 16 * gg + fc
                        if gi >= 2:
                            t.wait_ge(evS, b['ev'] + gi - 1)
                        for kc in range(16):
                            mi = t.matmul(
                                psb[gi % 2][:, :],
                                Bt(fc)[:, kc * 128:(kc + 1) * 128],
                                outs_g[gg % 2][:, kc * 512:(kc + 1) * 512],
                                start=(kc == 0), stop=(kc == 15))
                        mi.then_inc(mmS, 1)

            @blk.vector
            def _(v):
                for gg in range(4):
                    for fc in range(16):
                        gi = 16 * gg + fc
                        v.wait_ge(mmS, b['mm'] + gi + 1)
                        v.wait_ge(dX, b['dX'] + 16 * (20 * gg + 4 + fc + 1))
                        if gi >= 2:
                            v.wait_ge(dS, b['dS'] + 16 * (gi - 1))
                        v.tensor_tensor(otile[gi % 2], psb[gi % 2][:, :],
                                        xg[gi % 2], ALU.add)
                        v.tensor_tensor(otile[gi % 2], otile[gi % 2],
                                        Ct(fc)[:, gg * 512:(gg + 1) * 512],
                                        ALU.add).then_inc(evS, 1)

            @blk.scalar
            def _(s):
                for gg in range(4):
                    for fc in range(16):
                        gi = 16 * gg + fc
                        s.wait_ge(evS, b['ev'] + gi + 1)
                        s.dma_start(out_h[fc * 128:(fc + 1) * 128,
                                          gg * 512:(gg + 1) * 512],
                                    otile[gi % 2]).then_inc(dS, 16)
        N['dW'] += 256
        N['dX'] += 16 * 80
        N['mm'] += 64
        N['ev'] += 64
        N['dS'] += 64 * 16

        with nc.Block() as blk:
            @blk.sync
            def _(s):
                s.wait_ge(dS, N['dS'])

    return nc


_NC_CACHE = [None]


def kernel(**inputs):
    f32 = np.float32

    def lhsT_tiles(w):
        wt = np.ascontiguousarray(w.T)
        ink, out = wt.shape
        nk, nf = ink // 128, out // 128
        return np.ascontiguousarray(
            wt.reshape(nk, 128, nf, 128).transpose(2, 1, 0, 3))

    x = np.asarray(inputs['L0_attn_x'], f32)
    ins = dict(
        wx=lhsT_tiles(np.asarray(inputs['L0_attn_Wx'], f32)),
        ww=lhsT_tiles(np.asarray(inputs['L0_attn_Ww'], f32)),
        wk=lhsT_tiles(np.asarray(inputs['L0_attn_Wk'], f32)),
        wv=lhsT_tiles(np.asarray(inputs['L0_attn_Wv'], f32)),
        wr=lhsT_tiles(np.asarray(inputs['L0_attn_Wr'], f32)),
        wo=lhsT_tiles(np.asarray(inputs['L0_attn_Wo'], f32)),
        gate=np.ascontiguousarray(
            np.asarray(inputs['L0_ffn_gate_w'], f32).T.reshape(16, 128, 8)),
        gb=np.ascontiguousarray(
            np.asarray(inputs['L0_ffn_gate_bias'], f32).reshape(1, 8)),
        lns=np.ascontiguousarray(
            np.asarray(inputs['L0_attn_ln_scale'], f32).reshape(16, 128).T),
        lnb=np.ascontiguousarray(
            np.asarray(inputs['L0_attn_ln_bias'], f32).reshape(16, 128).T),
        bwv=np.ascontiguousarray(
            np.asarray(inputs['L0_attn_bw'], f32).reshape(16, 128).T),
        gws=lhsT_tiles(np.asarray(inputs['L0_ffn_shared_gw'], f32)),
        dws=lhsT_tiles(np.asarray(inputs['L0_ffn_shared_dw'], f32)),
        upw=np.stack([lhsT_tiles(np.asarray(inputs['L0_ffn_up_w'][e], f32))
                      for e in range(8)]),
        dnw=np.stack([lhsT_tiles(np.asarray(inputs['L0_ffn_down_w'][e], f32))
                      for e in range(8)]),
    )
    if _NC_CACHE[0] is None:
        _NC_CACHE[0] = build_nc()
    nc = _NC_CACHE[0]
    in_maps = []
    for core in range(8):
        m = dict(ins)
        m['x'] = np.ascontiguousarray(x[core].T)
        in_maps.append(m)
    res = run_bass_kernel_spmd(nc, in_maps, core_ids=list(range(8)))
    out = np.stack([np.ascontiguousarray(res.results[i]['out'].T)
                    for i in range(8)])
    return out


if __name__ == '__main__':
    import sys
    if '--build-only' in sys.argv:
        import time
        t0 = time.time()
        nc = build_nc()
        print(f"build ok in {time.time() - t0:.1f}s")


# version 10
# speedup vs baseline: 1.0181x; 1.0181x over previous
import numpy as np
import ml_dtypes
from contextlib import ExitStack
import concourse.bass as bass
import concourse.mybir as mybir
from concourse.bass_utils import run_bass_kernel_spmd

F32 = mybir.dt.float32
BF16 = mybir.dt.bfloat16
AF = mybir.ActivationFunctionType
ALU = mybir.AluOpType
AX = mybir.AxisListType

C = 2048
T = 2048
NCH = 16
EPS = 1e-5


def build_nc(debug=False):
    nc = bass.Bass("TRN2", target_bir_lowering=False)
    dbg = "ExternalOutput" if debug else "Internal"

    x_h = nc.dram_tensor("x", [C, T], F32, kind="ExternalInput").ap()
    wx_h = nc.dram_tensor("wx", [16, 128, 16, 128], F32, kind="ExternalInput").ap()
    ww_h = nc.dram_tensor("ww", [16, 128, 16, 128], F32, kind="ExternalInput").ap()
    wk_h = nc.dram_tensor("wk", [16, 128, 16, 128], F32, kind="ExternalInput").ap()
    wv_h = nc.dram_tensor("wv", [16, 128, 16, 128], F32, kind="ExternalInput").ap()
    wr_h = nc.dram_tensor("wr", [16, 128, 16, 128], F32, kind="ExternalInput").ap()
    wo_h = nc.dram_tensor("wo", [16, 128, 16, 128], F32, kind="ExternalInput").ap()
    gate_h = nc.dram_tensor("gate", [16, 128, 8], F32, kind="ExternalInput").ap()
    gb_h = nc.dram_tensor("gb", [1, 8], F32, kind="ExternalInput").ap()
    lns_h = nc.dram_tensor("lns", [128, 16], F32, kind="ExternalInput").ap()
    lnb_h = nc.dram_tensor("lnb", [128, 16], F32, kind="ExternalInput").ap()
    bw_h = nc.dram_tensor("bwv", [128, 16], F32, kind="ExternalInput").ap()
    gws_h = nc.dram_tensor("gws", [44, 128, 16, 128], F32, kind="ExternalInput").ap()
    dws_h = nc.dram_tensor("dws", [16, 128, 22, 128], F32, kind="ExternalInput").ap()
    upw_h = nc.dram_tensor("upw", [8, 22, 128, 16, 128], F32, kind="ExternalInput").ap()
    dnw_h = nc.dram_tensor("dnw", [8, 16, 128, 11, 128], F32, kind="ExternalInput").ap()
    out_h = nc.dram_tensor("out", [C, T], F32, kind="ExternalOutput").ap()

    k_scr = nc.dram_tensor("k_scr", [NCH, 128, 16, 128], BF16, kind=dbg).ap()
    v_scr = nc.dram_tensor("v_scr", [NCH, 128, 16, 128], BF16, kind=dbg).ap()
    r_scr = nc.dram_tensor("r_scr", [NCH, 128, 16, 128], BF16, kind=dbg).ap()
    sg_scr = nc.dram_tensor("sg_scr", [NCH, 128, 16, 128], F32, kind=dbg).ap()
    outs_scr = nc.dram_tensor("outs_scr", [NCH, 128, 16, 128], BF16, kind=dbg).ap()
    xn_scr = nc.dram_tensor("xn_scr", [16, 128, 2048], BF16, kind=dbg).ap()
    rows_hbm = nc.dram_tensor("rows_hbm", [8, 2048], F32, kind=dbg).ap()
    we_hbm = nc.dram_tensor("we_hbm", [8, 2048], F32, kind=dbg).ap()

    es = ExitStack()
    sbuf = lambda n, s, d: es.enter_context(nc.sbuf_tensor(n, s, d))
    psum = lambda n, s, d=F32: es.enter_context(nc.psum_tensor(n, s, d))
    sem = lambda n: es.enter_context(nc.semaphore(n))

    with es:
        A = sbuf("poolA", [128, 16 * 2048], BF16)
        B = sbuf("poolB", [128, 16 * 2048], BF16)
        Cp = sbuf("poolC", [128, 16 * 2048], BF16)
        Af = A[:, :].bitcast(F32)
        Bf = B[:, :].bitcast(F32)
        Cf = Cp[:, :].bitcast(F32)
        At = lambda i: A[:, i * 2048:(i + 1) * 2048]
        Bt = lambda i: B[:, i * 2048:(i + 1) * 2048]
        Ct = lambda i: Cp[:, i * 2048:(i + 1) * 2048]

        lns_t = sbuf("lns_t", [128, 16], F32)
        lnb_t = sbuf("lnb_t", [128, 16], F32)
        bwn_t = sbuf("bwn_t", [128, 16], F32)
        iop = sbuf("iop", [128, 1], F32)
        onesc = sbuf("onesc", [128, 1], BF16)
        onesf = sbuf("onesf", [128, 1], F32)
        we_rep = sbuf("we_rep", [128, 2048], F32)
        sgl = [sbuf("sgl0", [128, 512], F32), sbuf("sgl1", [128, 512], F32)]
        decay16 = sbuf("decay16", [128, 16], F32)

        psb = [psum(f"ps{i}", [128, 512], F32) for i in range(7)]
        psT = psum("psT", [128, 1024], BF16)

        dX = sem("dX")
        dW = sem("dW")
        dS = sem("dS")
        mmS = sem("mmS")
        scS = sem("scS")
        evS = sem("evS")
        N = dict(dX=0, dW=0, dS=0, mm=0, sc=0, ev=0)

        v3 = lambda ap2, k, t: ap2.rearrange("p (k t) -> p k t", k=k, t=t)

        with nc.Block() as blk:
            @blk.sync
            def _(s):
                s.dma_start(lns_t[:, :], lns_h[:, :]).then_inc(dX, 16)
                s.dma_start(lnb_t[:, :], lnb_h[:, :]).then_inc(dX, 16)
                s.dma_start(bwn_t[:, :], bw_h[:, :]).then_inc(dX, 16)

            @blk.gpsimd
            def _(g):
                g.iota(iop[:, :], [[0, 1]], channel_multiplier=1,
                       allow_small_or_imprecise_dtypes=True)
                g.memset(onesc[:, :], 1.0)
                g.memset(onesf[:, :], 1.0)
        N['dX'] += 48
        with nc.Block() as blk:
            @blk.vector
            def _(v):
                v.wait_ge(dX, N['dX'])
                v.tensor_scalar_mul(bwn_t[:, :], bwn_t[:, :], -1.0)

        row_t = Cf[0:1, 4096:8192]
        tmq = Cf[0:1, 12288:16384]
        xbuf = [Bf[:, 0:2048], Bf[:, 2048:4096]]
        xbbf = [Ct(0), Ct(1)]
        xqbf = [Ct(2), Ct(3)]
        b = dict(N)
        with nc.Block() as blk:
            @blk.sync
            def _(s):
                for fc in range(16):
                    if fc >= 2:
                        s.wait_ge(scS, b['sc'] + 2 * (fc - 1))
                    s.dma_start(xbuf[fc % 2], x_h[fc * 128:(fc + 1) * 128, :]
                                ).then_inc(dX, 16)

            @blk.scalar
            def _(s):
                for fc in range(16):
                    s.wait_ge(dX, b['dX'] + 16 * (fc + 1))
                    if fc >= 2:
                        s.wait_ge(mmS, b['mm'] + fc - 1)
                    s.activation(xbbf[fc % 2], xbuf[fc % 2], AF.Copy
                                 ).then_inc(scS, 1)
                    s.activation(xqbf[fc % 2], xbuf[fc % 2], AF.Square
                                 ).then_inc(scS, 1)

            @blk.tensor
            def _(t):
                for fc in range(16):
                    t.wait_ge(scS, b['sc'] + 2 * (fc + 1))
                    for tt in range(4):
                        t.matmul(psb[tt][0:1, :], onesc[:, :],
                                 xbbf[fc % 2][:, tt * 512:(tt + 1) * 512],
                                 start=(fc == 0), stop=(fc == 15),
                                 tile_position=(0, 0))
                        mi = t.matmul(psb[tt][32:33, :], onesc[:, :],
                                      xqbf[fc % 2][:, tt * 512:(tt + 1) * 512],
                                      start=(fc == 0), stop=(fc == 15),
                                      tile_position=(0, 32))
                    mi.then_inc(mmS, 1)
        N['dX'] += 256
        N['sc'] += 32
        N['mm'] += 16

        mu_row = row_t[0:1, 0:2048]
        var_row = row_t[0:1, 2048:4096]
        tA = tmq[0:1, 0:2048]
        tB = tmq[0:1, 2048:4096]

        def stat_finish(lo_rows, hi_rows, hbm_lo, hbm_hi):
            with nc.Block() as blk:
                @blk.scalar
                def _(s):
                    for tt in range(4):
                        s.mul(mu_row[0:1, tt * 512:(tt + 1) * 512],
                              lo_rows[tt], 1.0 / C)
                        s.mul(var_row[0:1, tt * 512:(tt + 1) * 512],
                              hi_rows[tt], 1.0 / C)
            with nc.Block() as blk:
                @blk.scalar
                def _(s):
                    s.activation(tA, mu_row, AF.Square)
            with nc.Block() as blk:
                @blk.vector
                def _(v):
                    v.tensor_sub(var_row, var_row, tA)
                    v.tensor_scalar_add(var_row, var_row, EPS)
            with nc.Block() as blk:
                @blk.scalar
                def _(s):
                    s.activation(tA, var_row, AF.Sqrt)
            with nc.Block() as blk:
                @blk.vector
                def _(v):
                    v.reciprocal(tB, tA)
            with nc.Block() as blk:
                @blk.vector
                def _(v):
                    v.tensor_mul(tA, tB, tB)
                    v.tensor_mul(tA, tA, var_row)
                    v.tensor_scalar(tA, tA, -0.5, 1.5, ALU.mult, ALU.add)
            with nc.Block() as blk:
                @blk.vector
                def _(v):
                    v.tensor_mul(var_row, tB, tA)
            with nc.Block() as blk:
                @blk.sync
                def _(s):
                    s.dma_start(hbm_lo, mu_row).then_inc(dX, 16)
                    s.dma_start(hbm_hi, var_row).then_inc(dX, 16)
                    N['dX'] += 32
                    s.wait_ge(dX, N['dX'])

        stat_finish([psb[tt][0:1, :] for tt in range(4)],
                    [psb[tt][32:33, :] for tt in range(4)],
                    rows_hbm[0:1, :], rows_hbm[1:2, :])

        mu_rep = Cf[:, 8192:10240]
        inv_rep = Cf[:, 10240:12288]
        with nc.Block() as blk:
            @blk.sync
            def _(s):
                s.dma_start(mu_rep, rows_hbm[0:1, :].broadcast_to([128, 2048])
                            ).then_inc(dX, 16)
                s.dma_start(inv_rep, rows_hbm[1:2, :].broadcast_to([128, 2048])
                            ).then_inc(dX, 16)
                N['dX'] += 32
                s.wait_ge(dX, N['dX'])

        gwt8 = Bf[:, 15040:15168]
        gb128 = Bf[:, 15184:15312]
        idf = Bf[:, 15312:15440]
        iof = Bf[:, 15440:15952]
        sc_fm = Bf[0:8, 12288:14336]
        sc_tm = Bf[:, 14336:14464]
        bsc = Bf[:, 14464:14592]
        m1c = Bf[:, 14592:14608]
        mskt = Bf[:, 14608:14736]
        m2c = Bf[:, 14736:14752]
        wselt = Bf[:, 14752:14880]
        den16 = Bf[:, 14880:14896]
        rden = Bf[:, 14896:14912]
        we_tm = Bf[:, 14912:15040]

        with nc.Block() as blk:
            @blk.gpsimd
            def _(g):
                g.iota(iof, [[0, 4], [1, 128]], channel_multiplier=0,
                       allow_small_or_imprecise_dtypes=True)
        with nc.Block() as blk:
            @blk.sync
            def _(s):
                s.dma_start(v3(gwt8, 16, 8),
                            gate_h.transpose([1, 0, 2])).then_inc(dX, 16)
                s.dma_start(v3(gb128, 16, 8),
                            gb_h.unsqueeze(1).broadcast_to([128, 16, 8])
                            ).then_inc(dX, 16)
        N['dX'] += 32
        with nc.Block() as blk:
            @blk.vector
            def _(v):
                v.wait_ge(dX, N['dX'])
                v.tensor_scalar(idf, iof[:, 0:128], iop[:, :], None,
                                ALU.is_equal, ALU.bypass)

        xnf = [Bf[:, 4096:6144], Bf[:, 6144:8192]]
        b = dict(N)
        with nc.Block() as blk:
            @blk.sync
            def _(s):
                for fc in range(16):
                    if fc >= 2:
                        s.wait_ge(evS, b['ev'] + 2 * (fc - 1))
                    s.dma_start(xbuf[fc % 2], x_h[fc * 128:(fc + 1) * 128, :]
                                ).then_inc(dX, 16)

            @blk.vector
            def _(v):
                for fc in range(16):
                    v.wait_ge(dX, b['dX'] + 16 * (fc + 1))
                    if fc >= 2:
                        v.wait_ge(scS, b['sc'] + fc - 1)
                        v.wait_ge(mmS, b['mm'] + fc - 1)
                    v.tensor_sub(xnf[fc % 2], xbuf[fc % 2], mu_rep)
                    v.tensor_mul(xnf[fc % 2], xnf[fc % 2], inv_rep
                                 ).then_inc(evS, 1)
                    v.tensor_copy(At(fc), xnf[fc % 2]).then_inc(evS, 1)

            @blk.scalar
            def _(s):
                for fc in range(16):
                    s.wait_ge(evS, b['ev'] + 2 * fc + 2)
                    s.activation(xqbf[fc % 2], xnf[fc % 2], AF.Square
                                 ).then_inc(scS, 1)
                    s.dma_start(xn_scr[fc], At(fc)).then_inc(dS, 16)

            @blk.tensor
            def _(t):
                for fc in range(16):
                    t.wait_ge(evS, b['ev'] + 2 * fc + 1)
                    t.wait_ge(scS, b['sc'] + fc + 1)
                    for tt in range(4):
                        t.matmul(psb[tt][0:8, :],
                                 gwt8[:, fc * 8:(fc + 1) * 8],
                                 xnf[fc % 2][:, tt * 512:(tt + 1) * 512],
                                 start=(fc == 0), stop=(fc == 15),
                                 tile_position=(0, 0))
                        t.matmul(psb[tt][64:65, :], onesf[:, :],
                                 xnf[fc % 2][:, tt * 512:(tt + 1) * 512],
                                 start=(fc == 0), stop=(fc == 15),
                                 tile_position=(0, 64))
                        mi = t.matmul(psb[tt][32:33, :], onesc[:, :],
                                      xqbf[fc % 2][:, tt * 512:(tt + 1) * 512],
                                      start=(fc == 0), stop=(fc == 15),
                                      tile_position=(0, 32))
                    mi.then_inc(mmS, 1)
        N['dX'] += 256
        N['ev'] += 32
        N['sc'] += 16
        N['dS'] += 256
        N['mm'] += 16

        with nc.Block() as blk:
            @blk.scalar
            def _(s):
                for tt in range(4):
                    s.activation(sc_fm[:, tt * 512:(tt + 1) * 512],
                                 psb[tt][0:8, :], AF.Sigmoid)
        stat_finish([psb[tt][64:65, :] for tt in range(4)],
                    [psb[tt][32:33, :] for tt in range(4)],
                    rows_hbm[2:3, :], rows_hbm[3:4, :])

        with nc.Block() as blk:
            @blk.tensor
            def _(t):
                for i in range(16):
                    t.matmul(psb[4][:, i * 8:(i + 1) * 8],
                             sc_fm[:, i * 128:(i + 1) * 128],
                             idf[0:8, 0:8], start=True, stop=True)
        with nc.Block() as blk:
            @blk.vector
            def _(v):
                v.tensor_copy(sc_tm, psb[4][:, 0:128])
        with nc.Block() as blk:
            @blk.vector
            def _(v):
                v.tensor_add(bsc, sc_tm, gb128)
        with nc.Block() as blk:
            @blk.vector
            def _(v):
                v.tensor_reduce(m1c, bsc.rearrange("p (i e) -> p i e", i=16),
                                AX.X, ALU.max)
        with nc.Block() as blk:
            @blk.vector
            def _(v):
                for i in range(16):
                    v.tensor_scalar(mskt[:, i * 8:(i + 1) * 8],
                                    bsc[:, i * 8:(i + 1) * 8],
                                    m1c[:, i:i + 1], None,
                                    ALU.is_equal, ALU.bypass)
        with nc.Block() as blk:
            @blk.vector
            def _(v):
                v.scalar_tensor_tensor(mskt, mskt, -1e30, bsc,
                                       ALU.mult, ALU.add)
        with nc.Block() as blk:
            @blk.vector
            def _(v):
                v.tensor_reduce(m2c, mskt.rearrange("p (i e) -> p i e", i=16),
                                AX.X, ALU.max)
        with nc.Block() as blk:
            @blk.vector
            def _(v):
                for i in range(16):
                    v.tensor_scalar(wselt[:, i * 8:(i + 1) * 8],
                                    bsc[:, i * 8:(i + 1) * 8],
                                    m2c[:, i:i + 1], None,
                                    ALU.is_ge, ALU.bypass)
        with nc.Block() as blk:
            @blk.vector
            def _(v):
                v.tensor_mul(wselt, wselt, sc_tm)
        with nc.Block() as blk:
            @blk.vector
            def _(v):
                v.tensor_reduce(den16, wselt.rearrange("p (i e) -> p i e", i=16),
                                AX.X, ALU.add)
        with nc.Block() as blk:
            @blk.vector
            def _(v):
                v.reciprocal(rden, den16)
        with nc.Block() as blk:
            @blk.vector
            def _(v):
                for i in range(16):
                    v.tensor_scalar(we_tm[:, i * 8:(i + 1) * 8],
                                    wselt[:, i * 8:(i + 1) * 8],
                                    rden[:, i:i + 1], None,
                                    ALU.mult, ALU.bypass)
        we_fmB = Bf[0:8, 12288:14336]
        for gix in range(4):
            with nc.Block() as blk:
                @blk.tensor
                def _(t):
                    for j in range(4):
                        i = gix * 4 + j
                        t.matmul(psb[5][0:8, j * 128:(j + 1) * 128],
                                 we_tm[:, i * 8:(i + 1) * 8],
                                 idf[:, 0:128], start=True, stop=True)
            with nc.Block() as blk:
                @blk.vector
                def _(v):
                    v.tensor_copy(we_fmB[:, gix * 512:(gix + 1) * 512],
                                  psb[5][0:8, :])
        with nc.Block() as blk:
            @blk.sync
            def _(s):
                s.dma_start(we_hbm[:, :], we_fmB).then_inc(dX, 16)
                N['dX'] += 16
                s.wait_ge(dX, N['dX'])

        mu2_rep = Cf[:, 8192:10240]
        inv2_rep = Cf[:, 10240:12288]
        with nc.Block() as blk:
            @blk.sync
            def _(s):
                s.dma_start(mu2_rep, rows_hbm[2:3, :].broadcast_to([128, 2048])
                            ).then_inc(dX, 16)
                s.dma_start(inv2_rep, rows_hbm[3:4, :].broadcast_to([128, 2048])
                            ).then_inc(dX, 16)
                N['dX'] += 32
                s.wait_ge(dX, N['dX'])
        with nc.Block() as blk:
            @blk.vector
            def _(v):
                for fc in range(16):
                    v.tensor_sub(we_rep[:, :], At(fc), mu2_rep)
                    v.tensor_mul(we_rep[:, :], we_rep[:, :], inv2_rep)
                    v.tensor_scalar(Bt(fc), we_rep[:, :],
                                    lns_t[:, fc:fc + 1], lnb_t[:, fc:fc + 1],
                                    ALU.mult, ALU.add)

        LPb = Af[:, 0:2048]
        mask4 = A[:, 4096:4608]
        identb = A[:, 4608:4736]
        ones_f = Af[:, 2368:2496]
        S_all = Af[:, 3072:4096]
        la_lo = Af[:, 4096:5120]
        la_hi = Af[:, 7168:8192]
        S_bf = A[:, 5 * 2048:5 * 2048 + 1024]
        outs_sb = At(6)
        smbf = [A[:, 10 * 2048:10 * 2048 + 512], A[:, 11 * 2048:11 * 2048 + 512]]
        khall = At(12)
        vtm = At(13)
        khtm = At(14)
        rtiall = At(15)

        _FB = [8 * 1024, 9 * 1024, 10496]
        _KB = [8 * 2048 + 1408, 9 * 2048 + 1408, 23040]

        def f_ekt(sl, j):
            off = [0, 32, 96, 192][j]
            return Af[:, _FB[sl] + off: _FB[sl] + off + 32 * (j + 1)]

        def f_er(sl):
            return Af[:, _FB[sl] + 320:_FB[sl] + 448]

        def f_eri(sl):
            return Af[:, _FB[sl] + 448:_FB[sl] + 576]

        def f_ekh(sl):
            return Af[:, _FB[sl] + 576:_FB[sl] + 704]

        def f_ktj(sl, j):
            off = [0, 32, 96, 192][j]
            return A[:, _KB[sl] + off: _KB[sl] + off + 32 * (j + 1)]

        def f_rt(sl):
            return A[:, _KB[sl] + 320: _KB[sl] + 448]

        kcb = [Ct(4), Ct(9)]
        rcb = [Ct(5), Ct(10)]
        vcb = [Ct(6), Ct(11)]
        sgb = Cf[:, 7 * 1024:9 * 1024]
        wbuf = [Ct(0), Ct(1)]
        ktile = [Cp[:, 2 * 2048:2 * 2048 + 512],
                 Cp[:, 2 * 2048 + 512:2 * 2048 + 1024]]
        sgtile = [Cf[:, 3 * 1024:3 * 1024 + 512],
                  Cf[:, 3 * 1024 + 512:3 * 1024 + 1024]]
        t1sl = lambda kc: Cp[:, 12 * 2048 + kc * 512:12 * 2048 + (kc + 1) * 512]
        la_sl = lambda ft: (la_lo if ft < 8 else la_hi)[
            :, (ft % 8) * 128:(ft % 8) * 128 + 128]

        mmQ = sem("mmQ")
        scQ = sem("scQ")
        evQ = sem("evQ")
        dQ = sem("dQ")

        EVC = 225
        evSbf = lambda c: EVC * c + 1
        evScan = lambda c, ft: EVC * c + 2 + ft
        _FBASE = [17, 65, 112, 159]
        evVtm = lambda c, r: EVC * c + 62 + r
        evFact = lambda c, ft: (EVC * c + _FBASE[ft // 4]
                                + 11 * (ft % 4 + 1))
        evMask = lambda c, q: EVC * c + ((64 + q) if q < 2 else
                                         (65 + 47 * (q // 2 - 1) + 46 + q % 2))
        evOuts = lambda c, oo: EVC * c + (207 if oo == 3 else
                                          65 + 47 * oo + 45)
        evKhtm = lambda c, r: EVC * c + 208 + r
        evUpdD = lambda c, q: EVC * c + 209 + 2 * (q + 1)
        scLa = lambda c: 99 * c + 2
        scExps = lambda c, ft: 99 * c + 2 + 6 * (ft + 1)
        scDecay = lambda c: 99 * c + 99
        mmVtm = lambda c, r: 24 * c + [2, 4][r]
        mmScore = lambda c, q: 24 * c + [1, 3, 5, 6, 8, 10, 11, 13][q]
        mmOuts = lambda c, oo: 24 * c + [7, 9, 12, 14][oo]
        mmKh = lambda c, r: 24 * c + 15 + r
        mmUpd = lambda c, q: 24 * c + 17 + q
        dQd = lambda c: 64 * (c + 1)

        with nc.Block() as blk:
            @blk.gpsimd
            def _(g):
                g.iota(Af[:, 2560:3072], [[0, 4], [1, 128]],
                       channel_multiplier=0,
                       allow_small_or_imprecise_dtypes=True)

            @blk.vector
            def _(v):
                v.memset(S_all, 0.0)
                v.memset(ones_f, 1.0)
                v.memset(psb[2][:, :], 0.0)
                v.memset(psb[3][:, :], 0.0)
        with nc.Block() as blk:
            @blk.vector
            def _(v):
                v.tensor_scalar(mask4, Af[:, 2560:3072], iop[:, :], None,
                                ALU.is_ge, ALU.bypass)
                v.tensor_scalar(identb, Af[:, 2560:2688], iop[:, :], None,
                                ALU.is_equal, ALU.bypass)

        def T_packets(ch):
            pk = []

            def vtmr(r):
                def f(t):
                    t.wait_ge(dQ, dQd(ch))
                    if ch >= 1:
                        t.wait_ge(evQ, EVC * ch)
                    if r == 1:
                        t.wait_ge(evQ, evVtm(ch, 0))

                    for i in range(8):
                        ft = r * 8 + i
                        mi = t.transpose(psT[:, i * 128:(i + 1) * 128],
                                         vcb[ch % 2][:, ft * 128:(ft + 1) * 128],
                                         identb)
                    mi.then_inc(mmQ, 1)
                return f

            def score(q):
                def f(t):
                    t.wait_ge(evQ, evFact(ch, 2 * q + 1))
                    if q >= 2:
                        t.wait_ge(evQ, evMask(ch, q - 2))
                    elif ch >= 1:
                        t.wait_ge(evQ, evMask(ch - 1, q + 6))
                    for hp in range(4):
                        hh = hp % 2
                        ftq = 2 * q + hp // 2
                        sl = ftq % 3
                        for j in range(4):
                            w = 32 * (j + 1)
                            mi = t.matmul(
                                psb[2 + q % 2][0:w, hp * 128 + 32 * j:
                                               hp * 128 + 32 * (j + 1)],
                                f_ktj(sl, j)[hh * 64:(hh + 1) * 64, :],
                                f_rt(sl)[hh * 64:(hh + 1) * 64,
                                         32 * j:32 * (j + 1)],
                                start=True, stop=True,
                                tile_position=(hh * 64, 0))
                    mi.then_inc(mmQ, 1)
                return f

            def outsg(oo):
                def f(t):
                    if oo >= 2:
                        t.wait_ge(evQ, evOuts(ch, oo - 2))
                    elif ch >= 1:
                        t.wait_ge(evQ, evOuts(ch - 1, oo + 2))
                        t.wait_ge(evQ, evUpdD(ch - 1, 6 + oo))
                    mi = None
                    for q in (2 * oo, 2 * oo + 1):
                        t.wait_ge(evQ, evMask(ch, q))
                        t.wait_ge(evQ, evVtm(ch, 1))
                        for hp in range(4):
                            head = 4 * q + hp
                            hh = hp % 2
                            ftq = 2 * q + hp // 2
                            slot = (q % 2) * 2 + hp // 2
                            t.matmul(
                                psb[4 + oo % 2][hh * 64:(hh + 1) * 64,
                                                slot * 128:(slot + 1) * 128],
                                vtm[:, head * 64:(head + 1) * 64],
                                smbf[q % 2][:, hp * 128:(hp + 1) * 128],
                                start=True, stop=False,
                                tile_position=(0, hh * 64))
                            mi = t.matmul(
                                psb[4 + oo % 2][hh * 64:(hh + 1) * 64,
                                                slot * 128:(slot + 1) * 128],
                                S_bf[hh * 64:(hh + 1) * 64,
                                     ftq * 64:(ftq + 1) * 64],
                                rtiall[hh * 64:(hh + 1) * 64,
                                       ftq * 128:(ftq + 1) * 128],
                                start=False, stop=True,
                                tile_position=(hh * 64, hh * 64))
                    mi.then_inc(mmQ, 1)
                return f

            def khr(r):
                def f(t):
                    t.wait_ge(evQ, evFact(ch, r * 8 + 7))
                    t.wait_ge(evQ, evVtm(ch, 1) if r == 0 else evKhtm(ch, 0))
                    for i in range(8):
                        ft = r * 8 + i
                        mi = t.transpose(psT[:, i * 128:(i + 1) * 128],
                                         khall[:, ft * 128:(ft + 1) * 128],
                                         identb)
                    mi.then_inc(mmQ, 1)
                return f

            def upd(q):
                def f(t):
                    t.wait_ge(evQ, evKhtm(ch, q // 4))
                    if q >= 2:
                        t.wait_ge(evQ, evUpdD(ch, q - 2))
                    else:
                        t.wait_ge(evQ, evOuts(ch, 2 + q))
                    for hp in range(4):
                        head = 4 * q + hp
                        hh = hp % 2
                        pp = hp // 2
                        mi = t.matmul(
                            psb[4 + q % 2][hh * 64:(hh + 1) * 64,
                                           (q // 2) * 128 + pp * 64:
                                           (q // 2) * 128 + pp * 64 + 64],
                            khtm[:, head * 64:(head + 1) * 64],
                            vtm[:, head * 64:(head + 1) * 64],
                            start=True, stop=True,
                            tile_position=(0, hh * 64))
                    mi.then_inc(mmQ, 1)
                return f

            pk.append(score(0))
            pk.append(vtmr(0))
            pk.append(score(1))
            pk.append(vtmr(1))
            pk.append(score(2))
            pk.append(score(3))
            pk.append(outsg(0))
            pk.append(score(4))
            pk.append(outsg(1))
            pk.append(score(5))
            pk.append(score(6))
            pk.append(outsg(2))
            pk.append(score(7))
            pk.append(outsg(3))
            pk.append(khr(0))
            pk.append(khr(1))
            for q in range(8):
                pk.append(upd(q))
            return pk

        SD = {}
        Dms = {}
        dsc = [N['dS']]

        def S_packets(ch):
            pk = []

            def la(half):
                def f(s):
                    if half == 0:
                        s.wait_ge(dQ, dQd(ch))
                        if ch >= 1:
                            s.wait_ge(evQ, evScan(ch - 1, 15))
                    dst = la_lo if half == 0 else la_hi
                    s.activation(dst, sgb[:, half * 1024:(half + 1) * 1024],
                                 AF.Ln).then_inc(scQ, 1)
                return f

            def exps(ft):
                def f(s):
                    s.wait_ge(evQ, evScan(ch, ft))
                    if ft >= 3:
                        s.wait_ge(evQ, evFact(ch, ft - 3))
                    lp = LPb[:, ft * 128:(ft + 1) * 128]
                    sl = ft % 3
                    for j in range(4):
                        w = 32 * (j + 1)
                        bias = (lp[:, 32 * j - 1:32 * j] if j > 0 else 0.0)
                        s.activation(f_ekt(sl, j), lp[:, 0:w], AF.Exp,
                                     scale=-1.0, bias=bias).then_inc(scQ, 1)
                    s.activation(f_eri(sl), lp, AF.Exp).then_inc(scQ, 1)
                    s.activation(f_ekh(sl), lp, AF.Exp, scale=-1.0,
                                 bias=lp[:, 127:128]).then_inc(scQ, 1)
                return f

            def decay(s):
                if ch >= 1:
                    s.wait_ge(evQ, evUpdD(ch - 1, 7))
                s.activation(decay16[:, :], LPb[:, 127:2048:128], AF.Exp
                             ).then_inc(scQ, 1)

            def odma(s):
                s.wait_ge(evQ, evOuts(ch, 3))
                s.dma_start(outs_scr[ch].rearrange("p f t -> p (f t)"),
                            outs_sb).then_inc(dS, 16)
                dsc[0] += 16
                SD[ch] = dsc[0]

            pk.append(la(0))
            pk.append(la(1))
            for ft in range(16):
                pk.append(exps(ft))
            pk.append(decay)
            pk.append(odma)
            return pk

        def LD_emit(s, ch):
            if ch >= 2:
                s.wait_ge(evQ, EVC * (ch - 1))
                s.wait_ge(mmQ, 24 * (ch - 1))
            if ch >= 1:
                s.wait_ge(scQ, scLa(ch - 1))
            s.wait_ge(dS, Dms[ch // 4])
            s.dma_start(kcb[ch % 2], k_scr[ch].rearrange("p f t -> p (f t)")
                        ).then_inc(dQ, 16)
            s.dma_start(rcb[ch % 2], r_scr[ch].rearrange("p f t -> p (f t)")
                        ).then_inc(dQ, 16)
            s.dma_start(vcb[ch % 2], v_scr[ch].rearrange("p f t -> p (f t)")
                        ).then_inc(dQ, 16)
            s.dma_start(sgb, sg_scr[ch].rearrange("p f t -> p (f t)")
                        ).then_inc(dQ, 16)

        def V_chunk(v, ch):
            if ch >= 1:
                v.wait_ge(mmQ, mmOuts(ch - 1, 3))
            v.tensor_copy(S_bf, S_all).then_inc(evQ, 1)
            v.wait_ge(scQ, scLa(ch))
            for ft in range(16):
                v.tensor_tensor_scan(LPb[:, ft * 128:(ft + 1) * 128], ones_f,
                                     la_sl(ft), 0.0, ALU.mult, ALU.add
                                     ).then_inc(evQ, 1)
            def v_oevac(oo):
                v.wait_ge(mmQ, mmOuts(ch, oo))
                if oo == 0 and ch >= 1:
                    v.wait_ge(dS, SD[ch - 1])
                v.tensor_copy(outs_sb[:, oo * 512:(oo + 1) * 512],
                              psb[4 + oo % 2][:, :]).then_inc(evQ, 1)

            for oo in range(4):
                for i in range(4):
                    ft = 4 * oo + i
                    v.wait_ge(scQ, scExps(ch, ft))
                    if ft >= 3:
                        v.wait_ge(mmQ, mmScore(ch, (ft - 3) // 2))
                    elif ch >= 1:
                        v.wait_ge(mmQ, 24 * (ch - 1) + 16)
                    sl = ft % 3
                    fsl = ft * 128
                    for j in range(4):
                        v.reciprocal(f_er(sl)[:, 32 * j:32 * (j + 1)],
                                     f_ekt(sl, j)[:, 32 * j:32 * (j + 1)]
                                     ).then_inc(evQ, 1)
                    v.tensor_tensor(f_rt(sl), rcb[ch % 2][:, fsl:fsl + 128],
                                    f_er(sl), ALU.mult).then_inc(evQ, 1)
                    for j in range(4):
                        v.tensor_tensor(f_ktj(sl, j),
                                        kcb[ch % 2][:, fsl:fsl + 32 * (j + 1)],
                                        f_ekt(sl, j), ALU.mult).then_inc(evQ, 1)
                    v.tensor_tensor(rtiall[:, fsl:fsl + 128],
                                    rcb[ch % 2][:, fsl:fsl + 128],
                                    f_eri(sl), ALU.mult).then_inc(evQ, 1)
                    v.tensor_tensor(khall[:, fsl:fsl + 128],
                                    kcb[ch % 2][:, fsl:fsl + 128],
                                    f_ekh(sl), ALU.mult).then_inc(evQ, 1)
                if oo == 0:
                    for r in range(2):
                        v.wait_ge(mmQ, mmVtm(ch, r))
                        v.tensor_copy(vtm[:, r * 1024:(r + 1) * 1024],
                                      psT[:, :]).then_inc(evQ, 1)
                else:
                    v_oevac(oo - 1)
                for q in (2 * oo, 2 * oo + 1):
                    v.wait_ge(mmQ, mmScore(ch, q))
                    if q >= 2:
                        v.wait_ge(mmQ, mmOuts(ch, (q - 2) // 2))
                    elif ch >= 1:
                        v.wait_ge(mmQ, mmOuts(ch - 1, (q + 6) // 2))
                    v.tensor_tensor(smbf[q % 2], psb[2 + q % 2][:, :],
                                    mask4, ALU.mult).then_inc(evQ, 1)
            v_oevac(3)
            for r in range(2):
                v.wait_ge(mmQ, mmKh(ch, r))
                v.tensor_copy(khtm[:, r * 1024:(r + 1) * 1024], psT[:, :]
                              ).then_inc(evQ, 1)
            v.wait_ge(scQ, scDecay(ch))
            for q in range(8):
                v.wait_ge(mmQ, mmUpd(ch, q))
                for pp in range(2):
                    ftq = 2 * q + pp
                    v.scalar_tensor_tensor(
                        S_all[:, ftq * 64:(ftq + 1) * 64],
                        S_all[:, ftq * 64:(ftq + 1) * 64],
                        decay16[:, ftq:ftq + 1],
                        psb[4 + q % 2][:, (q // 2) * 128 + pp * 64:
                                       (q // 2) * 128 + pp * 64 + 64],
                        ALU.mult, ALU.add).then_inc(evQ, 1)

        b4 = dict(N)
        Ws = [(wx_h, 't1', None), (ww_h, 'sg', sg_scr), (wk_h, 'copy', k_scr),
              (wv_h, 'copy', v_scr), (wr_h, 'sig', r_scr)]
        tile_ms = {}

        with nc.Block() as blk:
            @blk.gpsimd
            def _(g):
                for gg in range(4):
                    for wi in range(5):
                        w_src = Ws[wi][0]
                        for fc in range(16):
                            sidx = gg * 80 + wi * 16 + fc
                            if sidx >= 2:
                                g.wait_ge(mmS, b4['mm'] + sidx - 1)
                            g.dma_start(v3(wbuf[sidx % 2], 16, 128), w_src[fc]
                                        ).then_inc(dW, 16)

            @blk.scalar
            def _(s):
                spk = []
                sdone = [0]

                def spk_slot(slot):
                    while (sdone[0] < len(spk)
                           and spk[sdone[0]][0] <= slot):
                        spk[sdone[0]][1](s)
                        sdone[0] += 1

                SOFF = ([0, 1] + [2 + ft for ft in range(16)] + [18, 25])
                for gg in range(4):
                    if gg >= 1:
                        for c in range(4 * (gg - 1), 4 * gg):
                            base = 80 * gg + 20 * (c % 4)
                            pkts = S_packets(c)
                            spk.extend((base + SOFF[j], f)
                                       for j, f in enumerate(pkts))
                    for wi in range(5):
                        kind = Ws[wi][1]
                        scr_dst = Ws[wi][2]
                        for fc in range(16):
                            sidx = gg * 80 + wi * 16 + fc
                            slot = wi * 16 + fc
                            s.wait_ge(mmS, b4['mm'] + sidx + 1)
                            ps = psb[[0, 1, 6][sidx % 3]][:, :]
                            if kind == 't1':
                                s.activation(t1sl(fc), ps, AF.Copy
                                             ).then_inc(scS, 1)
                            else:
                                tl = (ktile if kind != 'sg' else sgtile)
                                tid = (kind != 'sg', sidx % 2)
                                if tid in tile_ms:
                                    s.wait_ge(dS, tile_ms[tid])
                                if kind == 'copy':
                                    s.activation(tl[sidx % 2], ps, AF.Copy
                                                 ).then_inc(scS, 1)
                                elif kind == 'sig':
                                    s.activation(tl[sidx % 2], ps, AF.Sigmoid
                                                 ).then_inc(scS, 1)
                                else:
                                    s.activation(tl[sidx % 2], ps, AF.Sigmoid,
                                                 scale=-1.0,
                                                 bias=bwn_t[:, fc:fc + 1]
                                                 ).then_inc(scS, 1)
                                s.wait_ge(scS, b4['sc'] + sidx + 1)
                                s.dma_start(
                                    scr_dst[4 * gg:4 * gg + 4, :, fc, :]
                                    .transpose([1, 0, 2]),
                                    v3(tl[sidx % 2], 4, 128)).then_inc(dS, 16)
                                dsc[0] += 16
                                tile_ms[tid] = dsc[0]
                            spk_slot(80 * gg + slot)
                    Dms[gg] = dsc[0]
                for c in range(12, 16):
                    spk.extend((0, f) for f in S_packets(c))
                spk_slot(10 ** 9)

            @blk.sync
            def _(s):
                for ch in range(NCH):
                    LD_emit(s, ch)

            @blk.tensor
            def _(t):
                tpk = []
                tdone = [0]

                def tpk_slot(gslot):
                    while (tdone[0] < len(tpk)
                           and tpk[tdone[0]][0] <= gslot):
                        tpk[tdone[0]][1](t)
                        tdone[0] += 1

                TOFF = [7, 8, 9, 10, 11, 13, 14, 15, 17, 17, 18, 20,
                        21, 24, 24, 24, 26, 26, 26, 26, 26, 26, 26, 26]
                for gg in range(4):
                    if gg >= 1:
                        for c in range(4 * (gg - 1), 4 * gg):
                            pkts = T_packets(c)
                            base = 80 * gg + 20 * (c % 4)
                            tpk.extend((base + TOFF[j], f)
                                       for j, f in enumerate(pkts))
                    for wi in range(5):
                        kind = Ws[wi][1]
                        for fc in range(16):
                            sidx = gg * 80 + wi * 16 + fc
                            slot = wi * 16 + fc
                            t.wait_ge(dW, b4['dW'] + 16 * (sidx + 1))
                            if sidx >= 3:
                                t.wait_ge(scS, b4['sc'] + sidx - 2)
                            if kind == 'sg':
                                t.wait_ge(scS, b4['sc'] + gg * 80 + 16)
                            for kc in range(16):
                                rhs = (t1sl(kc) if kind == 'sg'
                                       else Bt(kc)[:, gg * 512:(gg + 1) * 512])
                                mi = t.matmul(
                                    psb[[0, 1, 6][sidx % 3]][:, :],
                                    wbuf[sidx % 2][:, kc * 128:(kc + 1) * 128],
                                    rhs, start=(kc == 0), stop=(kc == 15))
                            mi.then_inc(mmS, 1)
                            tpk_slot(sidx)
                for c in range(12, 16):
                    tpk.extend((0, f) for f in T_packets(c))
                while tdone[0] < len(tpk):
                    tpk[tdone[0]][1](t)
                    tdone[0] += 1

            @blk.vector
            def _(v):
                for ch in range(NCH):
                    V_chunk(v, ch)

        N['dW'] += 16 * 320
        N['mm'] += 320
        N['sc'] += 320
        N['dS'] = dsc[0]

        with nc.Block() as blk:
            @blk.sync
            def _(s):
                s.wait_ge(dS, N['dS'])
                for fc in range(16):
                    s.dma_start(At(fc), xn_scr[fc]).then_inc(dX, 16)
                N['dX'] += 256
                s.wait_ge(dX, N['dX'])

        gbuf = [Bt(11), Bt(12)]
        ubuf = [Bt(13), Bt(14)]
        dbuf = [B[:, 15 * 2048:15 * 2048 + 1408],
                B[:, 11 * 2048:11 * 2048 + 1408]]

        def ffn_expert(eidx, g_src, y_src, dn_src, routed, first):
            b = dict(N)
            with nc.Block() as blk:
                @blk.sync
                def _(s):
                    if routed:
                        s.dma_start(we_rep[:, :],
                                    we_hbm[eidx:eidx + 1, :]
                                    .broadcast_to([128, 2048])).then_inc(dX, 16)

                @blk.gpsimd
                def _(g):
                    for j in range(11):
                        if j >= 2:
                            g.wait_ge(mmS, b['mm'] + 8 * (j - 1))
                        g.dma_start(v3(gbuf[j % 2], 16, 128), g_src(j)
                                    ).then_inc(dW, 16)
                        g.dma_start(v3(ubuf[j % 2], 16, 128), y_src(j)
                                    ).then_inc(dW, 16)

                @blk.tensor
                def _(t):
                    for j in range(11):
                        t.wait_ge(dW, b['dW'] + 32 * (j + 1))
                        for tt in range(4):
                            gi = 4 * j + tt
                            if gi >= 2:
                                t.wait_ge(scS, b['sc'] + gi - 1)
                                t.wait_ge(evS, b['ev'] + gi - 1)
                            for kc in range(16):
                                mi = t.matmul(
                                    psb[tt % 2][:, :],
                                    gbuf[j % 2][:, kc * 128:(kc + 1) * 128],
                                    At(kc)[:, tt * 512:(tt + 1) * 512],
                                    start=(kc == 0), stop=(kc == 15))
                            mi.then_inc(mmS, 1)
                            for kc in range(16):
                                mi = t.matmul(
                                    psb[2 + tt % 2][:, :],
                                    ubuf[j % 2][:, kc * 128:(kc + 1) * 128],
                                    At(kc)[:, tt * 512:(tt + 1) * 512],
                                    start=(kc == 0), stop=(kc == 15))
                            mi.then_inc(mmS, 1)

                @blk.scalar
                def _(s):
                    for j in range(11):
                        for tt in range(4):
                            gi = 4 * j + tt
                            s.wait_ge(mmS, b['mm'] + 2 * gi + 1)
                            s.activation(sgl[gi % 2][:, :],
                                         psb[tt % 2][:, :],
                                         AF.Silu).then_inc(scS, 1)

                @blk.vector
                def _(v):
                    if routed:
                        v.wait_ge(dX, b['dX'] + 16)
                    for j in range(11):
                        for tt in range(4):
                            gi = 4 * j + tt
                            v.wait_ge(mmS, b['mm'] + 2 * gi + 2)
                            v.wait_ge(scS, b['sc'] + gi + 1)
                            pu = psb[2 + tt % 2][:, :]
                            if routed:
                                v.tensor_mul(pu, pu,
                                             we_rep[:, tt * 512:(tt + 1) * 512])
                            v.tensor_tensor(
                                B[:, j * 2048 + tt * 512:
                                  j * 2048 + (tt + 1) * 512],
                                sgl[gi % 2][:, :], pu, ALU.mult
                            ).then_inc(evS, 1)
            N['dW'] += 32 * 11
            N['mm'] += 88
            N['sc'] += 44
            N['ev'] += 44
            if routed:
                N['dX'] += 16

            b = dict(N)
            with nc.Block() as blk:
                @blk.gpsimd
                def _(g):
                    for fc in range(16):
                        if fc >= 2:
                            g.wait_ge(mmS, b['mm'] + 4 * (fc - 1))
                        g.dma_start(v3(dbuf[fc % 2], 11, 128), dn_src(fc)
                                    ).then_inc(dW, 16)

                @blk.tensor
                def _(t):
                    for fc in range(16):
                        t.wait_ge(dW, b['dW'] + 16 * (fc + 1))
                        for tt in range(4):
                            gi = 4 * fc + tt
                            if gi >= 2:
                                t.wait_ge(evS, b['ev'] + gi - 1)
                            for kc in range(11):
                                mi = t.matmul(
                                    psb[4 + tt % 2][:, :],
                                    dbuf[fc % 2][:, kc * 128:(kc + 1) * 128],
                                    B[:, kc * 2048 + tt * 512:
                                      kc * 2048 + (tt + 1) * 512],
                                    start=(kc == 0), stop=(kc == 10))
                            mi.then_inc(mmS, 1)

                @blk.vector
                def _(v):
                    for fc in range(16):
                        for tt in range(4):
                            gi = 4 * fc + tt
                            v.wait_ge(mmS, b['mm'] + gi + 1)
                            dst = Ct(fc)[:, tt * 512:(tt + 1) * 512]
                            if first:
                                v.tensor_copy(dst, psb[4 + tt % 2][:, :]
                                              ).then_inc(evS, 1)
                            else:
                                v.tensor_tensor(dst, psb[4 + tt % 2][:, :],
                                                dst, ALU.add).then_inc(evS, 1)
            N['dW'] += 256
            N['mm'] += 64
            N['ev'] += 64

        for e in range(8):
            ffn_expert(e,
                       g_src=lambda j, e=e: upw_h[e, j],
                       y_src=lambda j, e=e: upw_h[e, 11 + j],
                       dn_src=lambda fc, e=e: dnw_h[e, fc],
                       routed=True, first=(e == 0))
        ffn_expert(8,
                   g_src=lambda j: gws_h[22 + j],
                   y_src=lambda j: gws_h[j],
                   dn_src=lambda fc: dws_h[fc, :, 0:11, :],
                   routed=False, first=False)
        ffn_expert(9,
                   g_src=lambda j: gws_h[33 + j],
                   y_src=lambda j: gws_h[11 + j],
                   dn_src=lambda fc: dws_h[fc, :, 11:22, :],
                   routed=False, first=False)

        outs_g = [A[:, 8 * 2048:12 * 2048], A[:, 12 * 2048:16 * 2048]]
        xg = [Af[:, 0:512], Af[:, 512:1024]]
        otile = [Af[:, 1024:1536], Af[:, 1536:2048]]
        b = dict(N)
        with nc.Block() as blk:
            @blk.gpsimd
            def _(g):
                for fc in range(16):
                    g.dma_start(v3(Bt(fc), 16, 128), wo_h[fc]).then_inc(dW, 16)

            @blk.sync
            def _(s):
                s.wait_ge(dS, b['dS'])
                for gg in range(4):
                    if gg >= 2:
                        s.wait_ge(mmS, b['mm'] + 16 * (gg - 1))
                    for cc in range(4):
                        s.dma_start(
                            outs_g[gg % 2].rearrange(
                                "p (k c t) -> p k c t", k=16, c=4, t=128
                            )[:, :, cc, :],
                            outs_scr[4 * gg + cc]).then_inc(dX, 16)
                    for fc in range(16):
                        gi = 16 * gg + fc
                        if gi >= 2:
                            s.wait_ge(evS, b['ev'] + gi - 1)
                        s.dma_start(xg[gi % 2],
                                    x_h[fc * 128:(fc + 1) * 128,
                                        gg * 512:(gg + 1) * 512]
                                    ).then_inc(dX, 16)

            @blk.tensor
            def _(t):
                t.wait_ge(dW, b['dW'] + 256)
                for gg in range(4):
                    t.wait_ge(dX, b['dX'] + 16 * (20 * gg + 4))
                    for fc in range(16):
                        gi = 16 * gg + fc
                        if gi >= 2:
                            t.wait_ge(evS, b['ev'] + gi - 1)
                        for kc in range(16):
                            mi = t.matmul(
                                psb[gi % 2][:, :],
                                Bt(fc)[:, kc * 128:(kc + 1) * 128],
                                outs_g[gg % 2][:, kc * 512:(kc + 1) * 512],
                                start=(kc == 0), stop=(kc == 15))
                        mi.then_inc(mmS, 1)

            @blk.vector
            def _(v):
                for gg in range(4):
                    for fc in range(16):
                        gi = 16 * gg + fc
                        v.wait_ge(mmS, b['mm'] + gi + 1)
                        v.wait_ge(dX, b['dX'] + 16 * (20 * gg + 4 + fc + 1))
                        if gi >= 2:
                            v.wait_ge(dS, b['dS'] + 16 * (gi - 1))
                        v.tensor_tensor(otile[gi % 2], psb[gi % 2][:, :],
                                        xg[gi % 2], ALU.add)
                        v.tensor_tensor(otile[gi % 2], otile[gi % 2],
                                        Ct(fc)[:, gg * 512:(gg + 1) * 512],
                                        ALU.add).then_inc(evS, 1)

            @blk.scalar
            def _(s):
                for gg in range(4):
                    for fc in range(16):
                        gi = 16 * gg + fc
                        s.wait_ge(evS, b['ev'] + gi + 1)
                        s.dma_start(out_h[fc * 128:(fc + 1) * 128,
                                          gg * 512:(gg + 1) * 512],
                                    otile[gi % 2]).then_inc(dS, 16)
        N['dW'] += 256
        N['dX'] += 16 * 80
        N['mm'] += 64
        N['ev'] += 64
        N['dS'] += 64 * 16

        with nc.Block() as blk:
            @blk.sync
            def _(s):
                s.wait_ge(dS, N['dS'])

    return nc


_NC_CACHE = [None]


def kernel(**inputs):
    f32 = np.float32

    def lhsT_tiles(w):
        wt = np.ascontiguousarray(w.T)
        ink, out = wt.shape
        nk, nf = ink // 128, out // 128
        return np.ascontiguousarray(
            wt.reshape(nk, 128, nf, 128).transpose(2, 1, 0, 3))

    x = np.asarray(inputs['L0_attn_x'], f32)
    ins = dict(
        wx=lhsT_tiles(np.asarray(inputs['L0_attn_Wx'], f32)),
        ww=lhsT_tiles(np.asarray(inputs['L0_attn_Ww'], f32)),
        wk=lhsT_tiles(np.asarray(inputs['L0_attn_Wk'], f32)),
        wv=lhsT_tiles(np.asarray(inputs['L0_attn_Wv'], f32)),
        wr=lhsT_tiles(np.asarray(inputs['L0_attn_Wr'], f32)),
        wo=lhsT_tiles(np.asarray(inputs['L0_attn_Wo'], f32)),
        gate=np.ascontiguousarray(
            np.asarray(inputs['L0_ffn_gate_w'], f32).T.reshape(16, 128, 8)),
        gb=np.ascontiguousarray(
            np.asarray(inputs['L0_ffn_gate_bias'], f32).reshape(1, 8)),
        lns=np.ascontiguousarray(
            np.asarray(inputs['L0_attn_ln_scale'], f32).reshape(16, 128).T),
        lnb=np.ascontiguousarray(
            np.asarray(inputs['L0_attn_ln_bias'], f32).reshape(16, 128).T),
        bwv=np.ascontiguousarray(
            np.asarray(inputs['L0_attn_bw'], f32).reshape(16, 128).T),
        gws=lhsT_tiles(np.asarray(inputs['L0_ffn_shared_gw'], f32)),
        dws=lhsT_tiles(np.asarray(inputs['L0_ffn_shared_dw'], f32)),
        upw=np.stack([lhsT_tiles(np.asarray(inputs['L0_ffn_up_w'][e], f32))
                      for e in range(8)]),
        dnw=np.stack([lhsT_tiles(np.asarray(inputs['L0_ffn_down_w'][e], f32))
                      for e in range(8)]),
    )
    if _NC_CACHE[0] is None:
        _NC_CACHE[0] = build_nc()
    nc = _NC_CACHE[0]
    in_maps = []
    for core in range(8):
        m = dict(ins)
        m['x'] = np.ascontiguousarray(x[core].T)
        in_maps.append(m)
    res = run_bass_kernel_spmd(nc, in_maps, core_ids=list(range(8)))
    out = np.stack([np.ascontiguousarray(res.results[i]['out'].T)
                    for i in range(8)])
    return out


if __name__ == '__main__':
    import sys
    if '--build-only' in sys.argv:
        import time
        t0 = time.time()
        nc = build_nc()
        print(f"build ok in {time.time() - t0:.1f}s")
